# version 24
# baseline (speedup 1.0000x reference)
import contextlib
import numpy as np

import concourse.bass as bass
import concourse.mybir as mybir
from concourse.ap import AP
from concourse.bass_utils import run_bass_kernel_spmd

F32 = mybir.dt.float32
BF16 = mybir.dt.bfloat16
AF = mybir.ActivationFunctionType
ALU = mybir.AluOpType
AXX = mybir.AxisListType.X

NCORES = 8
B, T, C = 2, 8192, 1024
NH, WIN = 16, 64
TOUT = 2048
COLS = 2304
NTB = 18
EB_PITCH = 192
EB_ROWS = 2177
EB_H = EB_ROWS * EB_PITCH
CHUNKS = [(128, 512), (640, 512), (1152, 512), (1664, 512), (2176, 128)]
C1 = 1.0 / C


class Plan:
    def __init__(self):
        self.ops = []
        self.cnt = {}

    def emit(self, eng, fn, waits=None, inc=None, amt=1):
        w = {k: v for k, v in (waits or {}).items() if v}
        self.ops.append([eng, w, fn, inc, amt])
        if inc is not None:
            self.cnt[inc] = self.cnt.get(inc, 0) + amt
        return self.cnt.get(inc, 0)

    def add_wait_last(self, sem, val):
        if val:
            w = self.ops[-1][1]
            w[sem] = max(w.get(sem, 0), val)

    def now(self, sem):
        return self.cnt.get(sem, 0)


def build_nc():
    nc = bass.Bass(trn_type="TRN2", target_bir_lowering=False, debug=False)

    d_xs = nc.dram_tensor("xs", [2112, C], F32, kind="ExternalInput").ap()
    dw = {}
    for nm, shp in [
        ("W_disp", [C, 256]), ("W_val", [C, C]), ("W_cproj", [C, C]),
        ("W_rT", [C, C]), ("W_kT", [C, C]), ("W_vT", [C, C]),
        ("BDs", [128, 128]), ("BDd", [128, 128]), ("BD32", [128, 256]),
        ("IDENT", [128, 128]), ("EMASK", [128, 252]),
        ("relT", [16, 64]), ("W_pos", [16, 16]),
        ("b_dmgC", [128, 1]), ("b_outC", [128, 1]), ("REP16", [16, 128]),
        ("mu_r", [128, 8]), ("mu_k", [128, 8]),
    ]:
        dw[nm] = nc.dram_tensor(nm, shp, F32, kind="ExternalInput").ap()
    d_out = nc.dram_tensor("out", [TOUT, C], F32, kind="ExternalOutput").ap()
    d_eb = nc.dram_tensor("Ebuf", [NH * EB_H], BF16, kind="Internal")
    d_x2 = nc.dram_tensor("X2buf", [2176, C], F32, kind="Internal")
    d_val = nc.dram_tensor("VALbuf", [COLS, C], BF16, kind="Internal")

    ctx = contextlib.ExitStack()
    def sbuf_in(cx, n, s, d, side=None):
        return cx.enter_context(nc.sbuf_tensor(n, s, d, side=side) if side else nc.sbuf_tensor(n, s, d))
    sbuf = lambda n, s, d: sbuf_in(ctx, n, s, d)
    psumt = lambda n: ctx.enter_context(nc.psum_tensor(n, [128, 512], F32))
    mksem = lambda n: ctx.enter_context(nc.semaphore(n))

    s_wdisp = sbuf("s_wdisp", [128, 2048], BF16)
    s_wval = sbuf("s_wval", [128, 8192], BF16)
    s_wcp = sbuf("s_wcp", [128, 8192], BF16)
    s_wrT = sbuf("s_wrT", [128, 8192], BF16)
    s_wkT = sbuf("s_wkT", [128, 8192], BF16)
    s_wvT = s_wval

    s_bds = sbuf("s_bds", [128, 128], BF16)
    s_bdd = sbuf("s_bdd", [128, 128], BF16)
    s_bd32 = sbuf("s_bd32", [128, 256], BF16)
    s_ident = sbuf("s_ident", [128, 128], BF16)
    s_emask = sbuf("s_emask", [128, 252], BF16)
    s_pcols = sbuf("s_pcols", [128, 64], F32)
    s_bdmgc = sbuf("s_bdmgc", [128, 1], F32)
    s_bout = sbuf("s_bout", [128, 1], F32)
    s_bouth = sbuf("s_bouth", [128, 1], F32)
    s_mur = sbuf("s_mur", [128, 8], F32)
    s_muk = sbuf("s_muk", [128, 8], F32)

    s_lnfm = sbuf("s_lnfm", [128, NTB * 1024], BF16)
    s_dS = sbuf("s_dS", [128, 2 * COLS], BF16)
    s_dD = sbuf("s_dD", [128, 2 * COLS], BF16)
    s_zrec = sbuf("s_zrec", [128, 17 * 16], F32)
    s_zsum = s_zrec
    s_lncol = sbuf("s_lncol", [128, 24], F32)
    s_cst = sbuf("s_cst", [128, 2], F32)
    ctx0 = contextlib.ExitStack()
    s_relT = sbuf_in(ctx0, "s_relT", [16, 64], F32)
    s_wpos = sbuf_in(ctx0, "s_wpos", [16, 16], F32)
    s_rep = sbuf_in(ctx0, "s_rep", [16, 128], F32)
    s_pfm = sbuf_in(ctx0, "s_pfm", [16, 64], F32)
    s_zero = sbuf_in(ctx0, "s_zero", [128, 2178], BF16)

    PS = [psumt(f"ps{i}") for i in range(8)]
    PSB = [PS[i][:, :].bitcast(BF16) for i in range(8)]

    sems = {n: mksem(n) for n in
            ["ld", "ld2", "ld3", "ld4", "ld5", "dx", "dxb", "dz", "de", "de2", "de3", "de4",
             "dr", "dr2", "dr3", "dr4", "dxr", "dxrb", "dx2", "ds", "dl", "do_",
             "dV", "dV2", "sv", "ss", "st"]}
    P = Plan()

    def wslices(sb_t, dram, nb):
        Fd = dram.shape[1]
        src = AP(dram.tensor, 0, [[Fd, 128], [128 * Fd, nb], [1, Fd]])
        dst = sb_t[:, :].rearrange("p (b f) -> p b f", b=nb)
        return dst, src

    for sb_t, nm in [(s_bds, "BDs"), (s_bdd, "BDd"), (s_bd32, "BD32"),
                     (s_ident, "IDENT"), (s_emask, "EMASK")]:
        P.emit("gpsimd", (lambda d=sb_t, nm=nm: (lambda e: e.dma_start(out=d[:, :], in_=dw[nm][:, :])))(),
               inc="ld", amt=16)
    ld_small = P.now("ld")
    ld_w = {}
    for (sb_t, nm), sm in zip([(s_wdisp, "W_disp"), (s_wval, "W_val"), (s_wcp, "W_cproj"),
                               (s_wrT, "W_rT"), (s_wkT, "W_kT")],
                              ("ld", "ld2", "ld3", "ld4", "ld5")):
        dst, src2 = wslices(sb_t, dw[nm], 8)
        P.emit("gpsimd", (lambda dst=dst, src2=src2: (lambda e: e.dma_start(out=dst, in_=src2)))(),
               inc=sm, amt=16)
        ld_w[nm] = {sm: P.now(sm)}
    for sb_t, nm in [(s_relT, "relT"), (s_wpos, "W_pos"), (s_bdmgc, "b_dmgC"),
                     (s_bout, "b_outC"), (s_rep, "REP16"),
                     (s_mur, "mu_r"), (s_muk, "mu_k")]:
        P.emit("sync", (lambda d=sb_t, nm=nm: (lambda e: e.dma_start(out=d[:, :], in_=dw[nm][:, :])))(),
               inc="dx", amt=16)
    ld_all = P.now("ld")
    dx_consts = P.now("dx")

    P.emit("vector", lambda e: e.memset(s_zero[:, :], 0.0), inc="sv")
    P.emit("vector", lambda e: e.memset(s_cst[:, 0:1], 1e-5), inc="sv")
    P.emit("vector", lambda e: e.memset(s_cst[:, 1:2], -5.0), inc="sv")
    P.emit("vector", lambda e: e.memset(s_lnfm[:, 0:1024], 0.0), inc="sv")
    for m in range(2):
        P.emit("vector", (lambda m=m: (lambda e: e.memset(s_dS[:, m * COLS:m * COLS + 128], 0.0)))(), inc="sv")
        P.emit("vector", (lambda m=m: (lambda e: e.memset(s_dD[:, m * COLS:m * COLS + 128], 0.0)))(), inc="sv")
    sv_memsets = P.now("sv")

    z1 = s_zero[:, 0:2176]
    for h in range(NH):
        off = h * EB_H
        P.emit("sync", (lambda off=off: (lambda e: e.dma_start(
            out=AP(d_eb.ap().tensor, off, [[1, 192]]), in_=s_zero[0:1, 0:192])))(),
            waits={"sv": sv_memsets} if h == 0 else None, inc="dz", amt=16)
        d2 = AP(d_eb.ap().tensor, off + 192 + 64, [[192, 2176], [1, 128]])
        P.emit("sync", (lambda d2=d2, z1=z1: (lambda e: e.dma_start(out=d2, in_=z1)))(),
               inc="dz", amt=16)
    dz_all = P.now("dz")

    P.emit("tensor", lambda e: e.matmul(PS[0][0:16, 0:64], s_wpos[:, :], s_relT[:, :], start=True, stop=True),
           waits={"dx": dx_consts}, inc="st")
    P.emit("vector", lambda e: e.tensor_copy(s_pfm[:, :], PS[0][0:16, 0:64]),
           waits={"st": P.now("st")}, inc="sv")
    P.emit("tensor", lambda e: e.matmul(PS[0][0:128, 64:128], s_rep[:, :], s_pfm[:, :], start=True, stop=True),
           waits={"sv": P.now("sv")}, inc="st")
    P.emit("vector", lambda e: e.tensor_copy(s_pcols[:, :], PS[0][0:128, 64:128]),
           waits={"st": P.now("st")}, inc="sv")
    P.emit("vector", lambda e: e.tensor_scalar(out=s_bouth[:, :], in0=s_bout[:, :],
                                               scalar1=0.5, scalar2=None, op0=ALU.mult),
           waits={"dx": dx_consts}, inc="sv")
    sv_p0 = P.now("sv")

    def emit_ln(xti, lnt, wv, ws, apply_extra=None, sqout=None, sq2=False):
        P.emit("vector", (lambda xti=xti: (lambda e: e.reduce_sum(s_lncol[:, 0:1], xti[:, :], axis=AXX)))(),
               waits=wv, inc="sv")
        if sq2:
            for h in range(2):
                P.emit("vector", (lambda xti=xti, h=h: (lambda e: e.tensor_mul(
                    PS[7][:, 0:512], xti[:, h * 512:h * 512 + 512], xti[:, h * 512:h * 512 + 512])))(),
                    waits=ws if h == 0 else None, inc="sv")
                P.emit("vector", (lambda h=h: (lambda e: e.reduce_sum(
                    s_lncol[:, 8 + h:9 + h], PS[7][:, 0:512], axis=AXX)))(), inc="sv")
            P.emit("vector", lambda e: e.scalar_tensor_tensor(
                out=s_lncol[:, 1:2], in0=s_lncol[:, 8:9], scalar=1.0,
                in1=s_lncol[:, 9:10], op0=ALU.mult, op1=ALU.add), inc="sv")
            P.emit("vector", lambda e: e.tensor_scalar(out=s_lncol[:, 2:3], in0=s_lncol[:, 0:1],
                                                       scalar1=C1, scalar2=None, op0=ALU.mult), inc="sv")
        else:
            P.emit("scalar", (lambda xti=xti, sqout=sqout: (lambda e: e.activation(out=sqout[:, :], in_=xti[:, :], func=AF.Square,
                                                                      accum_out=s_lncol[:, 1:2])))(),
                   waits=ws, inc="ss")
            P.emit("scalar", lambda e: e.activation(out=s_lncol[:, 2:3], in_=s_lncol[:, 0:1], func=AF.Copy, scale=C1),
                   waits={"sv": P.now("sv")}, inc="ss")
        P.emit("vector", lambda e: e.tensor_scalar(out=s_lncol[:, 3:4], in0=s_lncol[:, 1:2],
                                                   scalar1=C1, scalar2=None, op0=ALU.mult),
               waits=None if sq2 else {"ss": P.now("ss")}, inc="sv")
        P.emit("vector", lambda e: e.tensor_mul(s_lncol[:, 4:5], s_lncol[:, 2:3], s_lncol[:, 2:3]), inc="sv")
        P.emit("vector", lambda e: e.tensor_sub(s_lncol[:, 5:6], s_lncol[:, 3:4], s_lncol[:, 4:5]), inc="sv")
        P.emit("scalar", lambda e: e.activation(out=s_lncol[:, 6:7], in_=s_lncol[:, 5:6], func=AF.Sqrt, bias=s_cst[:, 0:1]),
               waits={"sv": P.now("sv")}, inc="ss")
        P.emit("vector", lambda e: e.reciprocal(s_lncol[:, 7:8], s_lncol[:, 6:7]),
               waits={"ss": P.now("ss")}, inc="sv")
        P.emit("vector", (lambda xti=xti, lnt=lnt: (lambda e: e.tensor_scalar(
            out=lnt[:, :], in0=xti[:, :], scalar1=s_lncol[:, 2:3], scalar2=s_lncol[:, 7:8],
            op0=ALU.subtract, op1=ALU.mult)))(),
            waits=merge_w({}, apply_extra or {}), inc="sv")
        return P.now("sv")

    ctx0.close()
    ctxC2 = contextlib.ExitStack()
    s_x2 = [sbuf_in(ctxC2, f"s_x2{i}", [128, C], F32, side="right") for i in range(2)]
    s_ytok0 = sbuf_in(ctxC2, "s_ytok0", [128, C], BF16, side="right")
    s_ytok2 = [s_ytok0, s_ytok0]
    s_ytok = s_ytok0
    ctxA = contextlib.ExitStack()
    s_xt = [sbuf_in(ctxA, f"s_xt{i}", [128, C], F32, side="right") for i in range(2)]
    s_lntok = [sbuf_in(ctxA, f"s_lntok{i}", [128, C], BF16, side="right") for i in range(2)]
    s_dscr = sbuf_in(ctxA, "s_dscr", [128, 256], BF16, side="right")
    s_vstage = [sbuf_in(ctxA, f"s_vstage{i}", [128, 1024], BF16, side="right") for i in range(2)]

    P.emit("vector", lambda e: e.memset(s_xt[1][0:64, :], 0.0), inc="sv")
    P.emit("vector", lambda e: e.memset(s_vstage[0][:, :], 0.0), inc="sv")
    sv_x1ms = P.now("sv")
    P.emit("sync", lambda e: e.dma_start(out=d_val.ap()[0:128, :], in_=s_vstage[0][:, :]),
           waits={"sv": sv_x1ms}, inc="dV", amt=16)
    vspill = {0: P.now("dV")}
    P.emit("sync", lambda e: e.dma_start(out=s_xt[1][64:128, :], in_=d_xs[0:64, :]),
           inc="dx", amt=16)
    xdma = {1: {"dx": P.now("dx")}}

    def merge_w(base, extra):
        w = dict(base)
        for k, v in extra.items():
            w[k] = max(w.get(k, 0), v)
        return w

    lnfm_evac, val_evac, dSD_evac, ln_done, tr_done = {}, {}, {}, {0: sv_p0}, {}
    sv_dscr = {}

    def part_ln(b):
        xti, lnt = s_xt[b % 2], s_lntok[b % 2]
        if b >= 2:
            r0 = 128 * (b - 2) + 64
            wl = {"sv": ln_done.get(b - 2, 0), "dz": dz_all if b == 2 else 0}
            P.emit("sync", (lambda xti=xti, r0=r0: (lambda e: e.dma_start(
                out=xti[0:64, :], in_=d_xs[r0:r0 + 64, :])))(),
                waits=wl, inc="dx", amt=16)
            P.emit("sync", (lambda xti=xti, r0=r0: (lambda e: e.dma_start(
                out=xti[64:128, :], in_=d_xs[r0 + 64:r0 + 128, :])))(),
                waits=wl, inc="dxb", amt=16)
            xdma[b] = {"dx": P.now("dx"), "dxb": P.now("dxb")}
        wv = merge_w({"sv": max(sv_memsets, sv_x1ms), "ss": P.now("ss")}, xdma[b])
        ws = merge_w({"sv": max(sv_memsets, sv_x1ms), "dz": dz_all if b == 1 else 0}, xdma[b])
        ap_extra = {"st": tr_done.get(b - 2, 0)}
        ln_done[b] = emit_ln(xti, lnt, wv, ws, ap_extra, sqout=s_x2[0])

    def part_te(b):
        lnt = s_lntok[b % 2]
        bank = b % 2
        wtp = {"sv": ln_done[b], "ss": max(lnfm_evac.get(b - 2, 0), val_evac.get(b - 1, 0)),
               "ld": ld_small, "dx": dx_consts}
        for cb in range(8):
            P.emit("tensor", (lambda lnt=lnt, bank=bank, cb=cb: (lambda e: e.transpose(
                PSB[bank][:, cb * 128:cb * 128 + 128], lnt[:, cb * 128:cb * 128 + 128], s_ident[:, :])))(),
                waits=wtp if cb == 0 else None, inc="st")
        tr_done[b] = P.now("st")
        P.emit("scalar", (lambda b=b, bank=bank: (lambda e: e.activation(
            out=s_lnfm[:, b * 1024:(b + 1) * 1024], in_=PSB[bank][:, 0:1024], func=AF.Copy)))(),
            waits={"st": P.now("st")}, inc="ss")
        lnfm_evac[b] = P.now("ss")

    def part_val(b):
        for eh in range(2):
            bank = eh
            wvv = merge_w({"ss": max(lnfm_evac[b], val_evac.get(b - 2, 0)),
                           "dV": vspill.get(b - 2, 0)}, ld_w["W_val"])
            for cb in range(8):
                P.emit("tensor", (lambda b=b, eh=eh, cb=cb, bank=bank: (lambda e: e.matmul(
                    PS[bank][:, :], s_lnfm[:, b * 1024 + cb * 128: b * 1024 + cb * 128 + 128],
                    s_wval[:, cb * 1024 + eh * 512: cb * 1024 + eh * 512 + 512],
                    start=(cb == 0), stop=(cb == 7))))(),
                    waits=wvv if cb == 0 else None, inc="st")
            P.emit("scalar", (lambda b=b, eh=eh, bank=bank: (lambda e: e.activation(
                out=s_vstage[b % 2][:, eh * 512: eh * 512 + 512], in_=PS[bank][:, :], func=AF.Copy)))(),
                waits={"st": P.now("st")}, inc="ss")
        val_evac[b] = P.now("ss")
        P.emit("sync", (lambda b=b: (lambda e: e.dma_start(
            out=d_val.ap()[128 * b:128 * b + 128, :], in_=s_vstage[b % 2][:, :])))(),
            waits={"ss": val_evac[b]}, inc="dV", amt=16)
        vspill[b] = P.now("dV")

    def part_disp(b):
        ev_prev = [0]
        for m in range(2):
            bank = 2
            wvv = merge_w({"ss": max(lnfm_evac[b], dSD_evac.get(b - 1, 0), ev_prev[0])},
                          ld_w["W_disp"])
            for cb in range(8):
                P.emit("tensor", (lambda b=b, m=m, cb=cb, bank=bank: (lambda e: e.matmul(
                    PS[bank][:, 0:128], s_wdisp[:, cb * 256 + m * 128: cb * 256 + m * 128 + 128],
                    s_lnfm[:, b * 1024 + cb * 128: b * 1024 + cb * 128 + 128],
                    start=(cb == 0), stop=(cb == 7))))(),
                    waits=wvv if cb == 0 else None, inc="st")
            P.emit("scalar", (lambda m=m, bank=bank: (lambda e: e.activation(
                out=s_dscr[:, m * 128:m * 128 + 128], in_=PS[bank][:, 0:128], func=AF.Copy)))(),
                waits={"st": P.now("st")}, inc="ss")
            ev_prev[0] = P.now("ss")
        sv_dscr[b] = P.now("ss")

    def part_dsd(b):
        ev_prev = [0]
        for m in range(2):
            P.emit("tensor", (lambda m=m: (lambda e: e.matmul(
                PS[3][:, 0:128], s_bds[:, :], s_dscr[:, m * 128:m * 128 + 128], start=True, stop=True)))(),
                waits={"ss": max(sv_dscr[b], ev_prev[0])}, inc="st")
            P.emit("tensor", (lambda m=m: (lambda e: e.matmul(
                PS[2][:, 0:128], s_bdd[:, :], s_dscr[:, m * 128:m * 128 + 128], start=True, stop=True)))(),
                inc="st")
            st_d = P.now("st")
            P.emit("scalar", (lambda b=b, m=m: (lambda e: e.activation(
                out=s_dS[:, m * COLS + 128 * b: m * COLS + 128 * b + 128], in_=PS[3][:, 0:128], func=AF.Copy)))(),
                waits={"st": st_d}, inc="ss")
            P.emit("scalar", (lambda b=b, m=m: (lambda e: e.activation(
                out=s_dD[:, m * COLS + 128 * b: m * COLS + 128 * b + 128], in_=PS[2][:, 0:128], func=AF.Copy)))(),
                inc="ss")
            ev_prev[0] = P.now("ss")
        dSD_evac[b] = P.now("ss")

    TPARTS = (part_ln, part_te, part_val, part_disp, part_dsd)
    TKEYS = (0.0, 1.0, 1.5, 2.0, 2.5)

    def emit_tile(b):
        for pf in TPARTS:
            pf(b)


    ctxB = contextlib.ExitStack()
    s_argT = sbuf_in(ctxB, "s_argT", [128, 6144], BF16)
    s_arg = [s_argT[:, 512 * i:512 * i + 512] for i in range(12)]
    s_actT = sbuf_in(ctxB, "s_actT", [128, 6144], BF16)
    s_sig2 = [sbuf_in(ctxB, f"s_sig2_{i}", [128, 512], F32) for i in range(2)]
    s_act = [s_actT[:, 512 * i:512 * i + 512] for i in range(12)]

    s_e = [[sbuf_in(ctxB, f"s_e{h}_{g}", [128, 512], BF16) for g in range(4)] for h in range(2)]
    s_etok0 = sbuf_in(ctxB, "s_etok0", [128, 1024], BF16)
    s_etok = [s_etok0, s_etok0]
    s_eh = sbuf_in(ctxB, "s_eh", [128, 4096], BF16)

    dump_cnt, compact_done = {}, {}
    DE_SEMS = ("de", "de2", "de3", "de4")
    LASTC = len(CHUNKS) - 1
    etok_guard = [0]
    SRC = {("b", 0): (s_dS, 0), ("b", 1): (s_dS, COLS), ("d", 0): (s_dD, 0), ("d", 1): (s_dD, COLS)}
    svG, ssG, stG = {}, {}, {}
    Gc = [0]
    emit_p4a_ref = [None]
    p4a_done_set = set()
    pop_hook = [lambda: None]

    bank_sv = {0: 0, 1: 0}
    bank_ss = {0: 0, 1: 0}

    def emit_chunk(ci, inter=()):
        inter = list(inter)
        G = Gc[0]
        c0, W = CHUNKS[ci]
        nts = W // 128
        tb_hi = min((c0 + W - 1) // 128, NTB - 1)

        def emit_comb(ht, g, k, stv):
            par = k % 2
            P.emit("scalar", (lambda par=par, W=W: (lambda e: e.activation(
                out=s_sig2[par][:, 0:W], in_=PS[6 + par][:, 0:W], func=AF.Tanh,
                scale=0.5, bias=s_bouth[:, 0:1])))(),
                waits={"st": stv, "sv": bank_sv[par]}, inc="ss")
            bank_ss[par] = P.now("ss")
            P.emit("vector", (lambda par=par, W=W: (lambda e: e.scalar_tensor_tensor(
                out=s_sig2[par][:, 0:W], in0=s_sig2[par][:, 0:W], scalar=-5.0,
                in1=PS[4 + par][:, 0:W],
                op0=ALU.mult, op1=ALU.add)))(),
                waits={"ss": P.now("ss"), "st": stv}, inc="sv")
            lgd = P.now("sv")
            bank_sv[par] = lgd
            P.emit("scalar", (lambda ht=ht, g=g, par=par, W=W: (lambda e: e.activation(
                out=s_e[ht][g][:, 0:W], in_=s_sig2[par][:, 0:W], func=AF.Exp,
                bias=s_cst[:, 1:2])))(),
                waits={"sv": lgd}, inc="ss")
            if ci == 0:
                P.emit("vector", (lambda ht=ht, g=g: (lambda e: e.tensor_mul(
                    s_e[ht][g][:, 128:191], s_e[ht][g][:, 128:191],
                    s_emask[:, g * 63:g * 63 + 63])))(),
                    waits={"ss": P.now("ss"), "ld": ld_small}, inc="sv")

        pend = []
        for k, (ht, g) in enumerate([(ht, g) for ht in range(2) for g in range(4)]):
            if len(pend) >= 2:
                emit_comb(*pend.pop(0))
                pop_hook[0]()
                if inter:
                    inter.pop(0)()
            for pi, path in enumerate(("b", "d")):
                srcT, srcOff = SRC[(path, ht)]
                bank = (4 + (k % 2)) if path == "b" else (6 + (k % 2))
                for jgl in range(4):
                    j0 = 16 * g + 4 * jgl
                    q = jgl
                    sbase = 4 * (G % 3)
                    wv = {}
                    if G - 3 in ssG:
                        wv["ss"] = ssG[G - 3]
                    if jgl == 0 and pi == 0 and k == 0:
                        wv["ss"] = max(wv.get("ss", 0), dSD_evac[min(tb_hi, NTB - 1)])
                    for v in range(4):
                        j = j0 + v
                        slot = sbase + v
                        P.emit("vector", (lambda srcT=srcT, srcOff=srcOff, c0=c0, W=W, j=j, slot=slot:
                                          (lambda e: e.tensor_sub(
                                              s_arg[slot][:, 0:W],
                                              srcT[:, srcOff + c0 - 63 + j: srcOff + c0 - 63 + j + W],
                                              srcT[:, srcOff + c0: srcOff + c0 + W])))(),
                               waits=wv if v == 0 else None, inc="sv" if v == 3 else None)
                    svG[G] = P.now("sv")
                    ws = {"sv": svG[G]}
                    if G - 3 in stG:
                        ws["st"] = stG[G - 3]
                    if path == "d" and W == 512:
                        half = G % 3
                        P.emit("scalar", (lambda half=half: (lambda e: e.activation(
                            out=s_actT[:, 2048 * half:2048 * half + 2048],
                            in_=s_argT[:, 2048 * half:2048 * half + 2048], func=AF.Gelu,
                            bias=s_bdmgc[:, 0:1])))(),
                            waits=ws, inc="ss")
                    else:
                        for v in range(4):
                            j = j0 + v
                            slot = sbase + v
                            bias = s_pcols[:, j:j + 1] if path == "b" else s_bdmgc[:, 0:1]
                            P.emit("scalar", (lambda slot=slot, W=W, bias=bias: (lambda e: e.activation(
                                out=s_act[slot][:, 0:W], in_=s_arg[slot][:, 0:W], func=AF.Gelu,
                                bias=bias)))(),
                                waits=ws if v == 0 else None, inc="ss" if v == 3 else None)
                    ssG[G] = P.now("ss")
                    wt = {"ss": ssG[G]}
                    if jgl == 0 and pi == 0:
                        wt["sv"] = max(bank_sv[k % 2], etok_guard[0] if k < 2 else 0)
                    if jgl == 0 and pi == 1:
                        wt["ss"] = max(ssG[G], bank_ss[k % 2])
                    for v in range(4):
                        slot = sbase + v
                        P.emit("tensor", (lambda bank=bank, q=q, v=v, slot=slot, W=W, pi=pi:
                                          (lambda e: e.matmul(
                                              PS[bank][32 * q:32 * q + 32, 0:W],
                                              s_bd32[:, pi * 128 + v * 32: pi * 128 + v * 32 + 32],
                                              s_act[slot][:, 0:W],
                                              start=(v == 0), stop=(v == 3),
                                              tile_position=(0, 32 * q))))(),
                               waits=wt if v == 0 else None, inc="st" if v == 3 else None)
                    stG[G] = P.now("st")
                    G += 1
                    Gc[0] = G
                pop_hook[0]()
            pend.append((ht, g, k, P.now("st")))
        for item in pend:
            emit_comb(*item)
            pop_hook[0]()
            if inter:
                inter.pop(0)()
        comb_sv, comb_ss = P.now("sv"), P.now("ss")
        compact_ts = {}
        css = {}
        for ts in range(nts):
            bank = 4 + ts % 2
            wtt = {"ss": comb_ss, "sv": max(comb_sv, compact_ts.get(ts - 2, 0))}
            for ht in range(2):
                for g in range(4):
                    P.emit("tensor", (lambda bank=bank, ht=ht, g=g, ts=ts: (lambda e: e.transpose(
                        PSB[bank][:, (ht * 4 + g) * 128:(ht * 4 + g) * 128 + 128],
                        s_e[ht][g][:, ts * 128:ts * 128 + 128], s_ident[:, :])))(),
                        waits=wtt if (ht == 0 and g == 0) else None, inc="st")
            P.emit("vector", (lambda ts=ts, bank=bank: (lambda e: e.tensor_copy(
                s_etok[ts % 2][:, :], PSB[bank][:, 0:1024])))(),
                waits={"st": P.now("st"), "ss": css.get(ts - 1, 0)}, inc="sv")
            sv_etok = P.now("sv")
            etok_guard[0] = sv_etok
            gtb = c0 // 128 + ts
            for hG in range(NH):
                ht, hh = hG // 8, hG % 8
                srcap = AP(s_etok[ts % 2][:, :].tensor, ht * 512 + hh, [[1024, 128], [128, 4], [8, 16]])
                zc = (gtb - 1) * 16 + hG
                wvv = {"sv": sv_etok}
                if ci > 0 and ts == 0:
                    wvv = merge_w(wvv, dump_cnt[ci - 1])
                P.emit("scalar", (lambda srcap=srcap, hG=hG, ts=ts, zc=zc: (lambda e: e.activation(
                    out=s_eh[:, hG * 256 + ts * 64: hG * 256 + ts * 64 + 64], in_=srcap,
                    func=AF.Copy, accum_out=s_zsum[:, zc:zc + 1])))(),
                    waits=wvv if hG == 0 else None, inc="ss")
            css[ts] = P.now("ss")
            P.emit("vector", (lambda gtb=gtb: (lambda e: e.reciprocal(
                s_zrec[:, (gtb - 1) * 16:(gtb - 1) * 16 + 16], s_zsum[:, (gtb - 1) * 16:(gtb - 1) * 16 + 16])))(),
                waits={"ss": css[ts]}, inc="sv")
            compact_ts[ts] = P.now("sv")
        compact_done[ci] = P.now("sv")
        cd_ss = P.now("ss")
        for hG in range(NH):
            off = hG * EB_H + (c0 - 127) * EB_PITCH
            dstp = AP(d_eb.ap().tensor, off, [[EB_PITCH, 128], [128 * EB_PITCH, nts], [1, 64]])
            srcp = s_eh[:, hG * 256: hG * 256 + nts * 64].rearrange("p (t j) -> p t j", j=64)
            P.emit("sync", (lambda dstp=dstp, srcp=srcp: (lambda e: e.dma_start(out=dstp, in_=srcp)))(),
                   waits={"ss": cd_ss, "dz": dz_all} if hG == 0 else None,
                   inc=DE_SEMS[hG % 4], amt=16)
        dump_cnt[ci] = {s: P.now(s) for s in DE_SEMS}
        for f in inter:
            f()

    for _, b, i in sorted((b + TKEYS[i], b, i) for b in range(1, 5) for i in range(5)):
        TPARTS[i](b)
    PQ = [(b, i) for b in range(5, NTB) for i in range(5)]
    PQc = [0]

    def pop_part():
        if PQc[0] < len(PQ):
            b, i = PQ[PQc[0]]
            PQc[0] += 1
            TPARTS[i](b)
    pop_hook[0] = pop_part
    emit_chunk(0)
    emit_chunk(1)
    emit_chunk(2)
    while PQc[0] < len(PQ):
        pop_part()
    pop_hook[0] = lambda: None
    p2_sv, p2_ss, p2_st = P.now("sv"), P.now("ss"), P.now("st")
    dst, src = wslices(s_wvT, dw["W_vT"], 8)
    P.emit("gpsimd", (lambda dst=dst, src=src: (lambda e: e.dma_start(out=dst, in_=src)))(),
           waits={"st": p2_st, "sv": p2_sv}, inc="ld", amt=16)
    ld_wvT = P.now("ld")
    ctxA.close()

    ctxC = contextlib.ExitStack()
    s_wbt = sbuf_in(ctxC, "s_wbt", [128, 4096], BF16, side="right")
    s_wb0 = sbuf_in(ctxC, "s_wb0", [128, 4096], BF16, side="right")
    s_wb = [s_wb0, s_wb0]
    s_atok0 = sbuf_in(ctxC, "s_atok0", [128, 1024], BF16, side="right")
    s_atok = [s_atok0, s_atok0]
    s_afm0 = s_atok0
    s_afm = [s_afm0, s_afm0]
    s_xrel0 = sbuf_in(ctxC, "s_xrel0", [128, C], F32, side="right")
    s_xrel = [s_xrel0, s_xrel0]
    s_valr = sbuf_in(ctxC, "s_valr", [128, 2048], BF16, side="right")

    P.emit("vector", lambda e: e.memset(s_wbt[:, 0:1], 0.0),
           waits={"ss": P.now("ss"), "st": P.now("st"), "sv": P.now("sv"),
                  "dV": vspill[NTB - 1], "dx": P.now("dx"), "dxb": P.now("dxb")}, inc="sv")
    for h in range(NH):
        P.emit("vector", (lambda h=h: (lambda e: e.memset(s_wbt[:, h * 256:h * 256 + 66], 0.0)))(), inc="sv")
    P.emit("vector", lambda e: e.memset(PS[0][:, :], 0.0), inc="sv")
    P.emit("vector", lambda e: e.memset(PS[1][:, :], 0.0), inc="sv")
    P.emit("vector", lambda e: e.memset(s_valr[:, 0:1024], 0.0), inc="sv")
    for cb in range(8):
        P.emit("vector", (lambda cb=cb: (lambda e: e.memset(s_lnfm[:, cb * 2304:cb * 2304 + 256], 0.0)))(), inc="sv")
    p4_init = P.now("sv")

    def chunk_of_col(c):
        for i, (cc0, WW) in enumerate(CHUNKS):
            if cc0 <= c < cc0 + WW:
                return i
        return LASTC

    rb_cnt, wbt_consumed, evacA, afm_evac, yfm_evac, spill_cnt, x2_ready = {}, {}, {}, {}, {}, {}, {}
    wb_evac_b, ytok_used = {}, {}

    DR_SEMS = ("dr", "dr2", "dr3", "dr4")

    def emit_readback(b):
        ci_hi = chunk_of_col(min(128 * b + 127, COLS - 1))
        wvr = merge_w({"dz": dz_all, "sv": p4_init,
                       "st": wbt_consumed.get(b - 1, 0)}, dump_cnt[ci_hi])
        off = (128 * (b - 1) + 1) * EB_PITCH
        rb = {}
        for hg in range(4):
            srcap = AP(d_eb.ap().tensor, off + hg * 4 * EB_H, [[191, 128], [EB_H, 4], [1, 191]])
            dstap = AP(s_wbt[:, :].tensor, 65 + hg * 1024, [[4096, 128], [256, 4], [1, 191]])
            P.emit("sync", (lambda dstap=dstap, srcap=srcap: (lambda e: e.dma_start(out=dstap, in_=srcap)))(),
                   waits=wvr if hg == 0 else None, inc=DR_SEMS[hg], amt=16)
            rb[hg] = (DR_SEMS[hg], P.now(DR_SEMS[hg]))
        rb_cnt[b] = rb

    emit_readback(1)
    vload = {}

    def emit_vload(b, wait_st):
        P.emit("sync", (lambda b=b: (lambda e: e.dma_start(
            out=s_valr[:, (b % 2) * 1024:(b % 2) * 1024 + 1024],
            in_=d_val.ap()[128 * b:128 * b + 128, :])))(),
            waits={"dV": vspill[b], "st": wait_st, "sv": p4_init}, inc="dV2", amt=16)
        vload[b] = P.now("dV2")

    emit_vload(1, 0)
    st_apply_b, dxr_ab = {}, {}

    def pa1(b):
        p4a_done_set.add(b)
        ring = b % 2
        hg_evac = {}
        for hg in range(4):
            bank = hg % 2
            wt = {rb_cnt[b][hg][0]: rb_cnt[b][hg][1], "sv": p4_init,
                  "ss": max(hg_evac.get(hg - 2, 0), wb_evac_b.get(b - 1, 0),
                            afm_evac.get(b - 1, 0))}
            for hh in range(4):
                h = hg * 4 + hh
                P.emit("tensor", (lambda bank=bank, h=h, hh=hh: (lambda e: e.transpose(
                    PSB[bank][:, hh * 256:hh * 256 + 128],
                    s_wbt[:, h * 256 + 128:h * 256 + 256], s_ident[:, :])))(),
                    waits=wt if hh == 0 else None, inc="st")
                P.emit("tensor", (lambda bank=bank, h=h, hh=hh: (lambda e: e.transpose(
                    PSB[bank][:, hh * 256 + 128:hh * 256 + 256],
                    s_wbt[:, h * 256:h * 256 + 128], s_ident[:, :])))(), inc="st")
            P.emit("scalar", (lambda ring=ring, hg=hg, bank=bank: (lambda e: e.activation(
                out=s_wb[ring][:, hg * 1024:hg * 1024 + 1024], in_=PSB[bank][:, 0:1024], func=AF.Copy)))(),
                waits={"st": P.now("st")}, inc="ss")
            hg_evac[hg] = P.now("ss")
        wbt_consumed[b] = P.now("st")
        wb_evac_b[b] = P.now("ss")
        if b + 1 < NTB:
            emit_readback(b + 1)

    def pa2(b):
        ring = b % 2
        pbank = 2
        wt = {"sv": max(evacA.get(b - 1, 0), x2_ready.get(b - 1, 0)),
              "ss": wb_evac_b[b], "dV2": vload[b]}
        for h in range(NH):
            eh = h // 8
            P.emit("tensor", (lambda ring=ring, b=b, h=h, eh=eh, pbank=pbank: (lambda e: e.matmul(
                PS[pbank + eh][:, (h % 8) * 64:(h % 8) * 64 + 64],
                s_wb[ring][:, h * 256:h * 256 + 128],
                s_valr[:, (b % 2) * 1024 + h * 64: (b % 2) * 1024 + h * 64 + 64],
                start=True, stop=False)))(),
                waits=wt if h == 0 else None, inc="st")
            P.emit("tensor", (lambda ring=ring, b=b, h=h, eh=eh, pbank=pbank: (lambda e: e.matmul(
                PS[pbank + eh][:, (h % 8) * 64:(h % 8) * 64 + 64],
                s_wb[ring][:, h * 256 + 128:h * 256 + 256],
                s_valr[:, ((b - 1) % 2) * 1024 + h * 64: ((b - 1) % 2) * 1024 + h * 64 + 64],
                start=False, stop=True)))(), inc="st")
        st_apply = P.now("st")
        st_apply_b[b] = st_apply
        if b + 1 < NTB:
            emit_vload(b + 1, st_apply)
        for eh in range(2):
            zap = AP(s_zrec[:, :].tensor, (b - 1) * 16 + eh * 8, [[272, 128], [1, 8], [0, 64]])
            P.emit("vector", (lambda ring=ring, eh=eh, pbank=pbank, zap=zap: (lambda e: e.tensor_tensor(
                s_atok[ring][:, eh * 512:eh * 512 + 512], PS[pbank + eh][:, :], zap, ALU.mult)))(),
                waits={"st": st_apply, "sv": compact_done[chunk_of_col(128 * b)]} if eh == 0 else None, inc="sv")
        evacA[b] = P.now("sv")

    def pa3(b):
        ring = b % 2
        wt = {"sv": evacA[b], "ss": wb_evac_b[b]}
        for heb in range(8):
            P.emit("tensor", (lambda ring=ring, heb=heb: (lambda e: e.transpose(
                PSB[0][:, heb * 128:heb * 128 + 128], s_atok[ring][:, heb * 128:heb * 128 + 128],
                s_ident[:, :])))(),
                waits=wt if heb == 0 else None, inc="st")
        P.emit("scalar", (lambda ring=ring: (lambda e: e.activation(
            out=s_afm[ring][:, :], in_=PSB[0][:, 0:1024], func=AF.Copy)))(),
            waits={"st": P.now("st")}, inc="ss")
        afm_evac[b] = P.now("ss")
        wvx = {"sv": max(x2_ready.get(b - 1, 0), p4_init), "ds": spill_cnt.get(b - 1, 0)}
        if b == 1:
            P.emit("vector", (lambda ring=ring: (lambda e: e.memset(s_xrel[ring][:, :], 0.0)))(),
                   waits=wvx, inc="sv")
            P.emit("sync", (lambda ring=ring: (lambda e: e.dma_start(out=s_xrel[ring][64:128, :],
                                                                     in_=d_xs[0:64, :])))(),
                   waits={"sv": P.now("sv")}, inc="dxr", amt=16)
        else:
            r0 = 128 * (b - 2) + 64
            P.emit("sync", (lambda ring=ring, r0=r0: (lambda e: e.dma_start(out=s_xrel[ring][0:64, :],
                                                                            in_=d_xs[r0:r0 + 64, :])))(),
                   waits=wvx, inc="dxr", amt=16)
            P.emit("sync", (lambda ring=ring, r0=r0: (lambda e: e.dma_start(out=s_xrel[ring][64:128, :],
                                                                            in_=d_xs[r0 + 64:r0 + 128, :])))(),
                   waits=wvx, inc="dxrb", amt=16)
        dxr_ab[b] = {"dxr": P.now("dxr"), "dxrb": P.now("dxrb")}

    def pa4(b):
        ring = b % 2
        for half in range(2):
            wt = merge_w({"sv": max(evacA[b], x2_ready.get(b - 1, 0)), "ss": afm_evac[b]},
                         ld_w["W_cproj"])
            for heb in range(8):
                P.emit("tensor", (lambda ring=ring, heb=heb, half=half: (lambda e: e.matmul(
                    PS[2 + half][:, :], s_afm[ring][:, heb * 128:heb * 128 + 128],
                    s_wcp[:, heb * 1024 + half * 512: heb * 1024 + half * 512 + 512],
                    start=(heb == 0), stop=(heb == 7))))(),
                    waits=wt if heb == 0 else None, inc="st")
            P.emit("vector", (lambda ring=ring, half=half: (lambda e: e.tensor_tensor(
                s_xrel[ring][:, half * 512:half * 512 + 512], PS[2 + half][:, :],
                s_xrel[ring][:, half * 512:half * 512 + 512], ALU.add)))(),
                waits=merge_w({"st": P.now("st"), "ss": P.now("ss"),
                               "ds": spill_cnt.get(b - 1, 0)}, dxr_ab[b]), inc="sv")
        sv_x2 = P.now("sv")
        x2_ready[b] = sv_x2
        P.emit("sync", (lambda ring=ring, b=b: (lambda e: e.dma_start(
            out=d_x2.ap()[128 * (b - 1):128 * (b - 1) + 128, :], in_=s_xrel[ring][:, :])))(),
            waits={"sv": sv_x2}, inc="ds", amt=16)
        spill_cnt[b] = P.now("ds")

    PAPARTS = (pa1, pa2, pa3, pa4)

    def emit_p4a(b):
        for pf in PAPARTS:
            pf(b)

    P4Q = [(b, i) for b in range(1, 13) for i in range(4)]
    P4Qc = [0]

    def pop_pa():
        if P4Qc[0] < len(P4Q):
            b, i = P4Q[P4Qc[0]]
            P4Qc[0] += 1
            PAPARTS[i](b)
    pop_hook[0] = pop_pa
    emit_chunk(3)
    emit_chunk(4)
    while P4Qc[0] < len(P4Q):
        pop_pa()
    pop_hook[0] = lambda: None
    p3_de = {s: P.now(s) for s in DE_SEMS}
    p3_sv, p3_ss, p3_st = P.now("sv"), P.now("ss"), P.now("st")
    ctxB.close()
    ln2_done = {}
    p4b_done = set()

    def emit_p4b_pair(b0, b1=None):
        bs = [b for b in (b0, b1) if b is not None and b not in p4b_done]
        if not bs:
            return
        dx2v = {}
        for b in bs:
            p4b_done.add(b)
            wvx = {"ds": spill_cnt[b], "sv": ln2_done.get(b - 2, 0), "ss": p2_ss}
            P.emit("gpsimd", (lambda b=b: (lambda e: e.dma_start(
                out=s_x2[b % 2][:, :], in_=d_x2.ap()[128 * (b - 1):128 * (b - 1) + 128, :])))(),
                waits=wvx, inc="dx2", amt=16)
            dx2v[b] = P.now("dx2")
        for b in bs:
            L = 12 * (b % 2)
            xti = s_x2[b % 2]
            P.emit("vector", (lambda xti=xti, L=L: (lambda e: e.reduce_sum(
                s_lncol[:, L:L + 1], xti[:, :], axis=AXX)))(),
                waits={"dx2": dx2v[b]}, inc="sv")
            for h in range(2):
                P.emit("vector", (lambda xti=xti, h=h: (lambda e: e.tensor_mul(
                    PS[7][:, 0:512], xti[:, h * 512:h * 512 + 512], xti[:, h * 512:h * 512 + 512])))(),
                    inc="sv")
                P.emit("vector", (lambda L=L, h=h: (lambda e: e.reduce_sum(
                    s_lncol[:, L + 8 + h:L + 9 + h], PS[7][:, 0:512], axis=AXX)))(), inc="sv")
            P.emit("vector", (lambda L=L: (lambda e: e.scalar_tensor_tensor(
                out=s_lncol[:, L + 1:L + 2], in0=s_lncol[:, L + 8:L + 9], scalar=1.0,
                in1=s_lncol[:, L + 9:L + 10], op0=ALU.mult, op1=ALU.add)))(), inc="sv")
            P.emit("vector", (lambda L=L: (lambda e: e.tensor_scalar(
                out=s_lncol[:, L + 2:L + 3], in0=s_lncol[:, L:L + 1],
                scalar1=C1, scalar2=None, op0=ALU.mult)))(), inc="sv")
            P.emit("vector", (lambda L=L: (lambda e: e.tensor_scalar(
                out=s_lncol[:, L + 3:L + 4], in0=s_lncol[:, L + 1:L + 2],
                scalar1=C1, scalar2=None, op0=ALU.mult)))(), inc="sv")
            P.emit("vector", (lambda L=L: (lambda e: e.tensor_mul(
                s_lncol[:, L + 4:L + 5], s_lncol[:, L + 2:L + 3], s_lncol[:, L + 2:L + 3])))(), inc="sv")
            P.emit("vector", (lambda L=L: (lambda e: e.tensor_sub(
                s_lncol[:, L + 5:L + 6], s_lncol[:, L + 3:L + 4], s_lncol[:, L + 4:L + 5])))(), inc="sv")
        Lmin = 12 * min(b % 2 for b in bs)
        nb = len(bs)
        sq_in = AP(s_lncol[:, :].tensor, Lmin + 5, [[24, 128], [12, nb]])
        sq_out = AP(s_lncol[:, :].tensor, Lmin + 6, [[24, 128], [12, nb]])
        rc_out = AP(s_lncol[:, :].tensor, Lmin + 7, [[24, 128], [12, nb]])
        P.emit("scalar", (lambda sq_in=sq_in, sq_out=sq_out: (lambda e: e.activation(
            out=sq_out, in_=sq_in, func=AF.Sqrt, bias=s_cst[:, 0:1])))(),
            waits={"sv": P.now("sv")}, inc="ss")
        P.emit("vector", (lambda sq_out=sq_out, rc_out=rc_out: (lambda e: e.reciprocal(rc_out, sq_out)))(),
               waits={"ss": P.now("ss")}, inc="sv")
        for b in bs:
            L = 12 * (b % 2)
            P.emit("vector", (lambda b=b, L=L: (lambda e: e.tensor_scalar(
                out=s_ytok2[b % 2][:, :], in0=s_x2[b % 2][:, :],
                scalar1=s_lncol[:, L + 2:L + 3], scalar2=s_lncol[:, L + 7:L + 8],
                op0=ALU.subtract, op1=ALU.mult)))(),
                waits={"st": ytok_used.get(b - 1, 0)}, inc="sv")
            ln2_done[b] = P.now("sv")
            wt = {"sv": ln2_done[b], "ss": max(p3_ss, yfm_evac.get(b - 1, 0))}
            for cb in range(8):
                P.emit("tensor", (lambda cb=cb, b=b: (lambda e: e.transpose(
                    PSB[6][:, cb * 128:cb * 128 + 128], s_ytok2[b % 2][:, cb * 128:cb * 128 + 128],
                    s_ident[:, :])))(),
                    waits=wt if cb == 0 else None, inc="st")
            ytok_used[b] = P.now("st")
            ydst = AP(s_lnfm[:, :].tensor, 128 * b, [[NTB * 1024, 128], [2304, 8], [1, 128]])
            P.emit("scalar", (lambda ydst=ydst: (lambda e: e.activation(
                out=ydst, in_=PSB[6][:, 0:1024], func=AF.Copy)))(),
                waits={"st": P.now("st")}, inc="ss")
            yfm_evac[b] = P.now("ss")

    tailb = [b for b in range(1, NTB) if b not in p4a_done_set]
    order = sorted(((ti + (0, 0.5, 0.9, 1.4)[i], ti, i) for ti in range(len(tailb)) for i in range(4)))
    pairs = [(1, 2), (3, 4), (5, 6), (7, 8), (9, None)]
    oi = 0
    for k in range(5):
        n = 4 if k < 4 else len(order) - oi
        for _ in range(n):
            _, ti, i = order[oi]
            oi += 1
            PAPARTS[i](tailb[ti])
        emit_p4b_pair(*pairs[k])
    p4a_sv, p4a_ss, p4a_st = P.now("sv"), P.now("ss"), P.now("st")
    p4_dr = {s: P.now(s) for s in DR_SEMS}
    p4_ds = P.now("ds")

    ctxC.close()

    ctxD = contextlib.ExitStack()
    s_xr2 = [sbuf_in(ctxD, f"s_xr{i}", [128, 4096], BF16) for i in range(2)]
    s_xk2 = [sbuf_in(ctxD, f"s_xk{i}", [128, 4096], BF16) for i in range(2)]
    s_kact2 = [sbuf_in(ctxD, f"s_kact{i}", [128, 4096], BF16) for i in range(2)]
    s_rtok4 = sbuf_in(ctxD, "s_rtok4", [128, 4096], BF16)
    s_tmp2 = sbuf_in(ctxD, "s_tmp2", [128, 512], BF16)
    s_ffn = sbuf_in(ctxD, "s_ffn", [128, 512], BF16)
    s_kr = sbuf_in(ctxD, "s_kr", [128, 512], F32)
    s_x2r0 = sbuf_in(ctxD, "s_x2r0", [128, C], F32)
    s_x2r = [s_x2r0, s_x2r0]

    kact_sv = {}
    rs4_ss = {}
    kv_v = {}
    sv_mix_f, ss_kact_f, sv_kact_f = {}, {}, {}
    k_used, r_used, kv_used = {}, {}, {}
    first_guard = merge_w(merge_w({"ss": p4a_ss, "ds": p4_ds,
                                   "sv": p4a_sv, "st": p4a_st}, p3_de), p4_dr)

    def emit_mix(fc):
        gc0 = 256 + 512 * fc
        b_hi = (gc0 + 511) // 128
        fp = fc % 2
        wv = {"ss": yfm_evac[b_hi], "ld": ld_wvT,
              "st": max(k_used.get(fc - 2, 0), r_used.get(fc - 2, 0))}
        if fc < 2:
            wv = merge_w(wv, first_guard)
        for cb in range(8):
            P.emit("vector", (lambda cb=cb, gc0=gc0: (lambda e: e.tensor_sub(
                s_tmp2[:, 0:512], s_lnfm[:, cb * 2304 + gc0: cb * 2304 + gc0 + 512],
                s_lnfm[:, cb * 2304 + gc0 - 1: cb * 2304 + gc0 + 511])))(),
                waits=wv if cb == 0 else None, inc="sv")
            P.emit("vector", (lambda cb=cb, gc0=gc0, fp=fp: (lambda e: e.scalar_tensor_tensor(
                out=s_xr2[fp][:, cb * 512:cb * 512 + 512], in0=s_tmp2[:, 0:512], scalar=s_mur[:, cb:cb + 1],
                in1=s_lnfm[:, cb * 2304 + gc0 - 1: cb * 2304 + gc0 + 511],
                op0=ALU.mult, op1=ALU.add)))(), inc="sv")
            P.emit("vector", (lambda cb=cb, gc0=gc0, fp=fp: (lambda e: e.scalar_tensor_tensor(
                out=s_xk2[fp][:, cb * 512:cb * 512 + 512], in0=s_tmp2[:, 0:512], scalar=s_muk[:, cb:cb + 1],
                in1=s_lnfm[:, cb * 2304 + gc0 - 1: cb * 2304 + gc0 + 511],
                op0=ALU.mult, op1=ALU.add)))(), inc="sv")
        sv_mix_f[fc] = P.now("sv")

    def emit_kph(fc):
        fp = fc % 2
        for fb in range(8):
            bank = 2 + fb % 2
            wt = merge_w({"sv": sv_mix_f[fc],
                          "ss": kact_sv.get((fc, fb - 2), kact_sv.get((fc - 1, fb + 6), 0))},
                         ld_w["W_kT"])
            for cb in range(8):
                P.emit("tensor", (lambda fb=fb, cb=cb, bank=bank, fp=fp: (lambda e: e.matmul(
                    PS[bank][:, :], s_wkT[:, cb * 1024 + fb * 128: cb * 1024 + fb * 128 + 128],
                    s_xk2[fp][:, cb * 512: cb * 512 + 512],
                    start=(cb == 0), stop=(cb == 7))))(),
                    waits=wt if cb == 0 else None, inc="st")
            P.emit("scalar", (lambda bank=bank: (lambda e: e.activation(
                out=s_kr[:, :], in_=PS[bank][:, :], func=AF.Relu)))(),
                waits={"st": P.now("st"), "ss": P.now("ss")}, inc="ss")
            P.emit("scalar", (lambda fb=fb, fp=fp: (lambda e: e.activation(
                out=s_kact2[fp][:, fb * 512:fb * 512 + 512], in_=s_kr[:, :], func=AF.Square)))(),
                waits={"st": kv_used.get(fc - 2, 0)} if fb == 0 else None, inc="ss")
            kact_sv[(fc, fb)] = P.now("ss")
        ss_kact_f[fc] = P.now("ss")
        sv_kact_f[fc] = P.now("sv")
        k_used[fc] = P.now("st")

    def emit_rph(fc):
        fp = fc % 2
        for tl in range(4):
            for eh in range(2):
                bank = 2 * (tl % 2) + eh
                wt = merge_w({"sv": sv_mix_f[fc],
                              "ss": max(rs4_ss.get((fc, tl - 2), rs4_ss.get((fc - 1, tl + 2), 0)),
                                        ss_kact_f[fc] if tl % 2 == 1 else 0)},
                             ld_w["W_rT"])
                for cb in range(8):
                    P.emit("tensor", (lambda tl=tl, eh=eh, cb=cb, bank=bank, fp=fp: (lambda e: e.matmul(
                        PS[bank][:, :], s_xr2[fp][:, cb * 512 + tl * 128: cb * 512 + tl * 128 + 128],
                        s_wrT[:, cb * 1024 + eh * 512: cb * 1024 + eh * 512 + 512],
                        start=(cb == 0), stop=(cb == 7))))(),
                        waits=wt if cb == 0 else None, inc="st")
                P.emit("scalar", (lambda tl=tl, eh=eh, bank=bank: (lambda e: e.activation(
                    out=s_rtok4[:, tl * 1024 + eh * 512: tl * 1024 + eh * 512 + 512],
                    in_=PS[bank][:, :], func=AF.Sigmoid)))(),
                    waits={"st": P.now("st"), "sv": kv_v.get((fc - 1, tl), 0)}, inc="ss")
            rs4_ss[(fc, tl)] = P.now("ss")
        r_used[fc] = P.now("st")

    def emit_kvph(fc):
        fp = fc % 2
        for tl in range(4):
            bglob = 2 + 4 * fc + tl
            ring = tl % 2
            wvx = {"ds": spill_cnt[bglob - 1], "sv": P.now("sv"),
                   "do_": P.now("do_")}
            P.emit("sync", (lambda ring=ring, bglob=bglob: (lambda e: e.dma_start(
                out=s_x2r[ring][:, :], in_=d_x2.ap()[128 * (bglob - 1):128 * (bglob - 1) + 128, :])))(),
                waits=wvx, inc="dl", amt=16)
            dl_t = P.now("dl")
            for eh in range(2):
                bank = 4 + eh
                wt = {"sv": max(sv_kact_f[fc], kv_v.get((fc, tl - 1), kv_v.get((fc - 1, 3), 0))),
                      "ss": max(rs4_ss[(fc, tl)], ss_kact_f[fc]), "ld": ld_wvT}
                for fb in range(8):
                    P.emit("tensor", (lambda tl=tl, eh=eh, fb=fb, bank=bank, fp=fp: (lambda e: e.matmul(
                        PS[bank][:, :], s_kact2[fp][:, fb * 512 + tl * 128: fb * 512 + tl * 128 + 128],
                        s_wvT[:, fb * 1024 + eh * 512: fb * 1024 + eh * 512 + 512],
                        start=(fb == 0), stop=(fb == 7))))(),
                        waits=wt if fb == 0 else None, inc="st")
                P.emit("vector", (lambda tl=tl, eh=eh, bank=bank: (lambda e: e.tensor_mul(
                    s_ffn[:, 0:512], PS[bank][:, :],
                    s_rtok4[:, tl * 1024 + eh * 512: tl * 1024 + eh * 512 + 512])))(),
                    waits={"st": P.now("st"), "ss": rs4_ss[(fc, tl)]}, inc="sv")
                P.emit("vector", (lambda ring=ring, eh=eh: (lambda e: e.tensor_add(
                    s_x2r[ring][:, eh * 512:eh * 512 + 512], s_ffn[:, 0:512],
                    s_x2r[ring][:, eh * 512:eh * 512 + 512])))(),
                    waits={"dl": dl_t, "do_": P.now("do_")}, inc="sv")
            kv_v[(fc, tl)] = P.now("sv")
            P.emit("sync", (lambda ring=ring, bglob=bglob: (lambda e: e.dma_start(
                out=d_out[128 * (bglob - 2):128 * (bglob - 2) + 128, :], in_=s_x2r[ring][:, :])))(),
                waits={"sv": P.now("sv")}, inc="do_", amt=16)
        kv_used[fc] = P.now("st")

    emit_mix(0)
    emit_kph(0)
    emit_p4b_pair(10, 11)
    emit_rph(0)
    emit_mix(1)
    emit_p4b_pair(12, 13)
    emit_kvph(0)
    emit_kph(1)
    emit_p4b_pair(14, 15)
    emit_rph(1)
    emit_mix(2)
    emit_p4b_pair(16, 17)
    emit_kvph(1)
    emit_kph(2)
    emit_rph(2)
    emit_mix(3)
    emit_kvph(2)
    emit_kph(3)
    emit_rph(3)
    emit_kvph(3)
    do_all = P.now("do_")
    P.emit("sync", lambda e: e.wait_ge(sems["do_"], do_all))
    ctxD.close()
    ctxC2.close()

    OWN = {"vector": "sv", "scalar": "ss", "tensor": "st"}
    with nc.Block() as block:
        def make_runner(engname):
            def run(eng):
                lastw = {}
                own = OWN.get(engname)
                own_cnt = 0
                dma_cnt = {}
                for (e, waits, fn, incn, amt) in P.ops:
                    if e != engname:
                        continue
                    if own is not None and own_cnt > lastw.get(own, 0) and waits.get(own, 0) < own_cnt:
                        waits = dict(waits)
                        waits[own] = own_cnt
                    if own is None and incn is not None:
                        prev = dma_cnt.get(incn, 0)
                        if prev > waits.get(incn, 0):
                            waits = dict(waits)
                            waits[incn] = prev
                        dma_cnt[incn] = prev + amt
                    for s, v in waits.items():
                        if v > lastw.get(s, 0):
                            eng.wait_ge(sems[s], v)
                            lastw[s] = v
                    inst = fn(eng)
                    if incn is not None and inst is not None:
                        inst.then_inc(sems[incn], amt)
                        if incn == own:
                            own_cnt += amt
            return run
        block.vector(make_runner("vector"))
        block.scalar(make_runner("scalar"))
        block.tensor(make_runner("tensor"))
        block.sync(make_runner("sync"))
        block.gpsimd(make_runner("gpsimd"))

    ctx.close()
    return nc


def _host_consts(inp):
    c = {}
    bds = np.zeros((128, 128), np.float32)
    bdd = np.zeros((128, 128), np.float32)
    for h in range(8):
        bds[16 * h:16 * h + 16, 16 * h:16 * h + 16] = inp["L0_attn_W_strain"]
        bdd[16 * h:16 * h + 16, 16 * h:16 * h + 16] = inp["L0_attn_W_dmg"]
    c["BDs"], c["BDd"] = bds, bdd
    bd32 = np.zeros((128, 256), np.float32)
    for pi, W in enumerate((inp["L0_attn_W_bond"], inp["L0_attn_W_dmg_out"])):
        for v in range(4):
            for h in range(8):
                bd32[h * 16:h * 16 + 16, pi * 128 + v * 32 + v * 8 + h] = W[:, 0]
    c["BD32"] = bd32
    c["IDENT"] = np.eye(128, dtype=np.float32)
    c["relT"] = np.ascontiguousarray(inp["L0_attn_rel_pos_emb"][:64].T)
    c["W_pos"] = inp["L0_attn_W_pos"]
    c["b_dmgC"] = np.ascontiguousarray(np.tile(inp["L0_attn_b_dmg"].reshape(16, 1), (8, 1)))
    c["b_outC"] = np.ascontiguousarray(np.broadcast_to(inp["L0_attn_b_dmg_out"].reshape(1, 1), (128, 1)).copy())
    rep = np.zeros((16, 128), np.float32)
    for d in range(16):
        rep[d, d::16] = 1.0
    c["REP16"] = rep
    c["mu_r"] = np.ascontiguousarray(inp["L0_ffn_mu_r"].reshape(8, 128).T)
    c["mu_k"] = np.ascontiguousarray(inp["L0_ffn_mu_k"].reshape(8, 128).T)
    c["W_disp"] = inp["L0_attn_W_disp"]
    c["W_val"] = inp["L0_attn_W_val"]
    c["W_cproj"] = inp["L0_attn_W_cproj"]
    c["W_rT"] = np.ascontiguousarray(inp["L0_ffn_W_r"].T)
    c["W_kT"] = np.ascontiguousarray(inp["L0_ffn_W_k"].T)
    c["W_vT"] = np.ascontiguousarray(inp["L0_ffn_W_v"].T)
    return c


def kernel(**inputs):
    inputs = {k: np.ascontiguousarray(np.asarray(v), dtype=np.float32) for k, v in inputs.items()}
    x = inputs["L0_attn_x"]
    consts = _host_consts(inputs)
    mask_edge = np.zeros((128, 252), np.float32)
    for g in range(4):
        for js in range(16):
            for h in range(8):
                p = 8 * js + h
                j = 16 * g + js
                for trel in range(63):
                    mask_edge[p, g * 63 + trel] = 1.0 if j >= 63 - trel else 0.0
    mask_ones = np.ones((128, 252), np.float32)

    in_maps = []
    for core in range(NCORES):
        bb, s0 = core // 4, (core % 4) * 2048
        xs = np.zeros((2112, C), np.float32)
        xs[64:] = x[bb, s0:s0 + 2048]
        if s0 > 0:
            xs[:64] = x[bb, s0 - 64:s0]
        m = dict(consts)
        m["xs"] = xs
        m["EMASK"] = mask_edge if s0 == 0 else mask_ones
        in_maps.append(m)

    nc = build_nc()
    res = run_bass_kernel_spmd(nc, in_maps, core_ids=list(range(NCORES)))
    out = np.zeros((B, T, C), np.float32)
    for core in range(NCORES):
        bb, s0 = core // 4, (core % 4) * 2048
        out[bb, s0:s0 + 2048] = res.results[core]["out"]
    return out


# version 25
# speedup vs baseline: 1.0178x; 1.0178x over previous
import contextlib
import numpy as np

import concourse.bass as bass
import concourse.mybir as mybir
from concourse.ap import AP
from concourse.bass_utils import run_bass_kernel_spmd

F32 = mybir.dt.float32
BF16 = mybir.dt.bfloat16
AF = mybir.ActivationFunctionType
ALU = mybir.AluOpType
AXX = mybir.AxisListType.X

NCORES = 8
B, T, C = 2, 8192, 1024
NH, WIN = 16, 64
TOUT = 2048
COLS = 2304
NTB = 18
EB_PITCH = 192
EB_ROWS = 2177
EB_H = EB_ROWS * EB_PITCH
CHUNKS = [(128, 512), (640, 512), (1152, 512), (1664, 512), (2176, 128)]
C1 = 1.0 / C


class Plan:
    def __init__(self):
        self.ops = []
        self.cnt = {}

    def emit(self, eng, fn, waits=None, inc=None, amt=1):
        w = {k: v for k, v in (waits or {}).items() if v}
        self.ops.append([eng, w, fn, inc, amt])
        if inc is not None:
            self.cnt[inc] = self.cnt.get(inc, 0) + amt
        return self.cnt.get(inc, 0)

    def add_wait_last(self, sem, val):
        if val:
            w = self.ops[-1][1]
            w[sem] = max(w.get(sem, 0), val)

    def now(self, sem):
        return self.cnt.get(sem, 0)


def build_nc():
    nc = bass.Bass(trn_type="TRN2", target_bir_lowering=False, debug=False)

    d_xs = nc.dram_tensor("xs", [2112, C], F32, kind="ExternalInput").ap()
    dw = {}
    for nm, shp in [
        ("W_disp", [C, 256]), ("W_val", [C, C]), ("W_cproj", [C, C]),
        ("W_rT", [C, C]), ("W_kT", [C, C]), ("W_vT", [C, C]),
        ("BDs", [128, 128]), ("BDd", [128, 128]), ("BD32", [128, 256]),
        ("IDENT", [128, 128]), ("EMASK", [128, 252]),
        ("relT", [16, 64]), ("W_pos", [16, 16]),
        ("b_dmgC", [128, 1]), ("b_outC", [128, 1]), ("REP16", [16, 128]),
        ("mu_r", [128, 8]), ("mu_k", [128, 8]),
    ]:
        dw[nm] = nc.dram_tensor(nm, shp, F32, kind="ExternalInput").ap()
    d_out = nc.dram_tensor("out", [TOUT, C], F32, kind="ExternalOutput").ap()
    d_eb = nc.dram_tensor("Ebuf", [NH * EB_H], BF16, kind="Internal")
    d_x2 = nc.dram_tensor("X2buf", [2176, C], F32, kind="Internal")
    d_val = nc.dram_tensor("VALbuf", [COLS, C], BF16, kind="Internal")

    ctx = contextlib.ExitStack()
    def sbuf_in(cx, n, s, d, side=None):
        return cx.enter_context(nc.sbuf_tensor(n, s, d, side=side) if side else nc.sbuf_tensor(n, s, d))
    sbuf = lambda n, s, d: sbuf_in(ctx, n, s, d)
    psumt = lambda n: ctx.enter_context(nc.psum_tensor(n, [128, 512], F32))
    mksem = lambda n: ctx.enter_context(nc.semaphore(n))

    s_wdisp = sbuf("s_wdisp", [128, 2048], BF16)
    s_wval = sbuf("s_wval", [128, 8192], BF16)
    s_wcp = sbuf("s_wcp", [128, 8192], BF16)
    s_wrT = sbuf("s_wrT", [128, 8192], BF16)
    s_wkT = sbuf("s_wkT", [128, 8192], BF16)
    s_wvT = s_wval

    s_bds = sbuf("s_bds", [128, 128], BF16)
    s_bdd = sbuf("s_bdd", [128, 128], BF16)
    s_bd32 = sbuf("s_bd32", [128, 256], BF16)
    s_ident = sbuf("s_ident", [128, 128], BF16)
    s_emask = sbuf("s_emask", [128, 252], BF16)
    s_pcols = sbuf("s_pcols", [128, 64], F32)
    s_bdmgc = sbuf("s_bdmgc", [128, 1], F32)
    s_bout = sbuf("s_bout", [128, 1], F32)
    s_bouth = sbuf("s_bouth", [128, 1], F32)
    s_mur = sbuf("s_mur", [128, 8], F32)
    s_muk = sbuf("s_muk", [128, 8], F32)

    s_lnfm = sbuf("s_lnfm", [128, NTB * 1024], BF16)
    s_dS = sbuf("s_dS", [128, 2 * COLS], BF16)
    s_dD = sbuf("s_dD", [128, 2 * COLS], BF16)
    s_zrec = sbuf("s_zrec", [128, 17 * 16], F32)
    s_zsum = s_zrec
    s_lncol = sbuf("s_lncol", [128, 24], F32)
    s_cst = sbuf("s_cst", [128, 2], F32)
    ctx0 = contextlib.ExitStack()
    s_relT = sbuf_in(ctx0, "s_relT", [16, 64], F32)
    s_wpos = sbuf_in(ctx0, "s_wpos", [16, 16], F32)
    s_rep = sbuf_in(ctx0, "s_rep", [16, 128], F32)
    s_pfm = sbuf_in(ctx0, "s_pfm", [16, 64], F32)
    s_zero = sbuf_in(ctx0, "s_zero", [128, 2178], BF16)

    PS = [psumt(f"ps{i}") for i in range(8)]
    PSB = [PS[i][:, :].bitcast(BF16) for i in range(8)]

    sems = {n: mksem(n) for n in
            ["ld", "ld2", "ld3", "ld4", "ld5", "dx", "dxb", "dz", "de", "de2", "de3", "de4",
             "dr", "dr2", "dr3", "dr4", "dxr", "dxrb", "dx2", "ds", "dl", "do_",
             "dV", "dV2", "sv", "ss", "st"]}
    P = Plan()

    def wslices(sb_t, dram, nb):
        Fd = dram.shape[1]
        src = AP(dram.tensor, 0, [[Fd, 128], [128 * Fd, nb], [1, Fd]])
        dst = sb_t[:, :].rearrange("p (b f) -> p b f", b=nb)
        return dst, src

    for sb_t, nm in [(s_bds, "BDs"), (s_bdd, "BDd"), (s_bd32, "BD32"),
                     (s_ident, "IDENT"), (s_emask, "EMASK")]:
        P.emit("gpsimd", (lambda d=sb_t, nm=nm: (lambda e: e.dma_start(out=d[:, :], in_=dw[nm][:, :])))(),
               inc="ld", amt=16)
    ld_small = P.now("ld")
    ld_w = {}
    for (sb_t, nm), sm in zip([(s_wdisp, "W_disp"), (s_wval, "W_val"), (s_wcp, "W_cproj"),
                               (s_wrT, "W_rT"), (s_wkT, "W_kT")],
                              ("ld", "ld2", "ld3", "ld4", "ld5")):
        dst, src2 = wslices(sb_t, dw[nm], 8)
        P.emit("gpsimd", (lambda dst=dst, src2=src2: (lambda e: e.dma_start(out=dst, in_=src2)))(),
               inc=sm, amt=16)
        ld_w[nm] = {sm: P.now(sm)}
    for sb_t, nm in [(s_relT, "relT"), (s_wpos, "W_pos"), (s_bdmgc, "b_dmgC"),
                     (s_bout, "b_outC"), (s_rep, "REP16"),
                     (s_mur, "mu_r"), (s_muk, "mu_k")]:
        P.emit("sync", (lambda d=sb_t, nm=nm: (lambda e: e.dma_start(out=d[:, :], in_=dw[nm][:, :])))(),
               inc="dx", amt=16)
    ld_all = P.now("ld")
    dx_consts = P.now("dx")

    P.emit("vector", lambda e: e.memset(s_zero[:, :], 0.0), inc="sv")
    P.emit("vector", lambda e: e.memset(s_cst[:, 0:1], 1e-5), inc="sv")
    P.emit("vector", lambda e: e.memset(s_cst[:, 1:2], -5.0), inc="sv")
    P.emit("vector", lambda e: e.memset(s_lnfm[:, 0:1024], 0.0), inc="sv")
    for m in range(2):
        P.emit("vector", (lambda m=m: (lambda e: e.memset(s_dS[:, m * COLS:m * COLS + 128], 0.0)))(), inc="sv")
        P.emit("vector", (lambda m=m: (lambda e: e.memset(s_dD[:, m * COLS:m * COLS + 128], 0.0)))(), inc="sv")
    sv_memsets = P.now("sv")

    z1 = s_zero[:, 0:2176]
    for h in range(NH):
        off = h * EB_H
        P.emit("sync", (lambda off=off: (lambda e: e.dma_start(
            out=AP(d_eb.ap().tensor, off, [[1, 192]]), in_=s_zero[0:1, 0:192])))(),
            waits={"sv": sv_memsets} if h == 0 else None, inc="dz", amt=16)
        d2 = AP(d_eb.ap().tensor, off + 192 + 64, [[192, 2176], [1, 128]])
        P.emit("sync", (lambda d2=d2, z1=z1: (lambda e: e.dma_start(out=d2, in_=z1)))(),
               inc="dz", amt=16)
    dz_all = P.now("dz")

    P.emit("tensor", lambda e: e.matmul(PS[0][0:16, 0:64], s_wpos[:, :], s_relT[:, :], start=True, stop=True),
           waits={"dx": dx_consts}, inc="st")
    P.emit("vector", lambda e: e.tensor_copy(s_pfm[:, :], PS[0][0:16, 0:64]),
           waits={"st": P.now("st")}, inc="sv")
    P.emit("tensor", lambda e: e.matmul(PS[0][0:128, 64:128], s_rep[:, :], s_pfm[:, :], start=True, stop=True),
           waits={"sv": P.now("sv")}, inc="st")
    P.emit("vector", lambda e: e.tensor_copy(s_pcols[:, :], PS[0][0:128, 64:128]),
           waits={"st": P.now("st")}, inc="sv")
    P.emit("vector", lambda e: e.tensor_scalar(out=s_bouth[:, :], in0=s_bout[:, :],
                                               scalar1=0.5, scalar2=None, op0=ALU.mult),
           waits={"dx": dx_consts}, inc="sv")
    sv_p0 = P.now("sv")

    def emit_ln(xti, lnt, wv, ws, apply_extra=None, sqout=None, sq2=False):
        P.emit("vector", (lambda xti=xti: (lambda e: e.reduce_sum(s_lncol[:, 0:1], xti[:, :], axis=AXX)))(),
               waits=wv, inc="sv")
        if sq2:
            for h in range(2):
                P.emit("vector", (lambda xti=xti, h=h: (lambda e: e.tensor_mul(
                    PS[7][:, 0:512], xti[:, h * 512:h * 512 + 512], xti[:, h * 512:h * 512 + 512])))(),
                    waits=ws if h == 0 else None, inc="sv")
                P.emit("vector", (lambda h=h: (lambda e: e.reduce_sum(
                    s_lncol[:, 8 + h:9 + h], PS[7][:, 0:512], axis=AXX)))(), inc="sv")
            P.emit("vector", lambda e: e.scalar_tensor_tensor(
                out=s_lncol[:, 1:2], in0=s_lncol[:, 8:9], scalar=1.0,
                in1=s_lncol[:, 9:10], op0=ALU.mult, op1=ALU.add), inc="sv")
            P.emit("vector", lambda e: e.tensor_scalar(out=s_lncol[:, 2:3], in0=s_lncol[:, 0:1],
                                                       scalar1=C1, scalar2=None, op0=ALU.mult), inc="sv")
        else:
            P.emit("scalar", (lambda xti=xti, sqout=sqout: (lambda e: e.activation(out=sqout[:, :], in_=xti[:, :], func=AF.Square,
                                                                      accum_out=s_lncol[:, 1:2])))(),
                   waits=ws, inc="ss")
            P.emit("scalar", lambda e: e.activation(out=s_lncol[:, 2:3], in_=s_lncol[:, 0:1], func=AF.Copy, scale=C1),
                   waits={"sv": P.now("sv")}, inc="ss")
        P.emit("vector", lambda e: e.tensor_scalar(out=s_lncol[:, 3:4], in0=s_lncol[:, 1:2],
                                                   scalar1=C1, scalar2=None, op0=ALU.mult),
               waits=None if sq2 else {"ss": P.now("ss")}, inc="sv")
        P.emit("vector", lambda e: e.tensor_mul(s_lncol[:, 4:5], s_lncol[:, 2:3], s_lncol[:, 2:3]), inc="sv")
        P.emit("vector", lambda e: e.tensor_sub(s_lncol[:, 5:6], s_lncol[:, 3:4], s_lncol[:, 4:5]), inc="sv")
        P.emit("scalar", lambda e: e.activation(out=s_lncol[:, 6:7], in_=s_lncol[:, 5:6], func=AF.Sqrt, bias=s_cst[:, 0:1]),
               waits={"sv": P.now("sv")}, inc="ss")
        P.emit("vector", lambda e: e.reciprocal(s_lncol[:, 7:8], s_lncol[:, 6:7]),
               waits={"ss": P.now("ss")}, inc="sv")
        P.emit("vector", (lambda xti=xti, lnt=lnt: (lambda e: e.tensor_scalar(
            out=lnt[:, :], in0=xti[:, :], scalar1=s_lncol[:, 2:3], scalar2=s_lncol[:, 7:8],
            op0=ALU.subtract, op1=ALU.mult)))(),
            waits=merge_w({}, apply_extra or {}), inc="sv")
        return P.now("sv")

    ctx0.close()
    ctxC2 = contextlib.ExitStack()
    s_x2 = [sbuf_in(ctxC2, f"s_x2{i}", [128, C], F32, side="right") for i in range(2)]
    s_ytok0 = sbuf_in(ctxC2, "s_ytok0", [128, C], BF16, side="right")
    s_ytok2 = [s_ytok0, s_ytok0]
    s_ytok = s_ytok0
    ctxA = contextlib.ExitStack()
    s_xt = [sbuf_in(ctxA, f"s_xt{i}", [128, C], F32, side="right") for i in range(2)]
    s_lntok = [sbuf_in(ctxA, f"s_lntok{i}", [128, C], BF16, side="right") for i in range(2)]
    s_dscr = sbuf_in(ctxA, "s_dscr", [128, 256], BF16, side="right")
    s_vstage = [sbuf_in(ctxA, f"s_vstage{i}", [128, 1024], BF16, side="right") for i in range(2)]

    P.emit("vector", lambda e: e.memset(s_xt[1][0:64, :], 0.0), inc="sv")
    P.emit("vector", lambda e: e.memset(s_vstage[0][:, :], 0.0), inc="sv")
    sv_x1ms = P.now("sv")
    P.emit("sync", lambda e: e.dma_start(out=d_val.ap()[0:128, :], in_=s_vstage[0][:, :]),
           waits={"sv": sv_x1ms}, inc="dV", amt=16)
    vspill = {0: P.now("dV")}
    P.emit("sync", lambda e: e.dma_start(out=s_xt[1][64:128, :], in_=d_xs[0:64, :]),
           inc="dx", amt=16)
    xdma = {1: {"dx": P.now("dx")}}

    def merge_w(base, extra):
        w = dict(base)
        for k, v in extra.items():
            w[k] = max(w.get(k, 0), v)
        return w

    lnfm_evac, val_evac, dSD_evac, ln_done, tr_done = {}, {}, {}, {0: sv_p0}, {}
    sv_dscr = {}

    def part_ln(b):
        xti, lnt = s_xt[b % 2], s_lntok[b % 2]
        if b >= 2:
            r0 = 128 * (b - 2) + 64
            wl = {"sv": ln_done.get(b - 2, 0), "dz": dz_all if b == 2 else 0}
            P.emit("sync", (lambda xti=xti, r0=r0: (lambda e: e.dma_start(
                out=xti[0:64, :], in_=d_xs[r0:r0 + 64, :])))(),
                waits=wl, inc="dx", amt=16)
            P.emit("sync", (lambda xti=xti, r0=r0: (lambda e: e.dma_start(
                out=xti[64:128, :], in_=d_xs[r0 + 64:r0 + 128, :])))(),
                waits=wl, inc="dxb", amt=16)
            xdma[b] = {"dx": P.now("dx"), "dxb": P.now("dxb")}
        wv = merge_w({"sv": max(sv_memsets, sv_x1ms), "ss": P.now("ss")}, xdma[b])
        ws = merge_w({"sv": max(sv_memsets, sv_x1ms), "dz": dz_all if b == 1 else 0}, xdma[b])
        ap_extra = {"st": tr_done.get(b - 2, 0)}
        ln_done[b] = emit_ln(xti, lnt, wv, ws, ap_extra, sqout=s_x2[0])

    def part_te(b):
        lnt = s_lntok[b % 2]
        bank = b % 2
        wtp = {"sv": ln_done[b], "ss": max(lnfm_evac.get(b - 2, 0), val_evac.get(b - 1, 0)),
               "ld": ld_small, "dx": dx_consts}
        for cb in range(8):
            P.emit("tensor", (lambda lnt=lnt, bank=bank, cb=cb: (lambda e: e.transpose(
                PSB[bank][:, cb * 128:cb * 128 + 128], lnt[:, cb * 128:cb * 128 + 128], s_ident[:, :])))(),
                waits=wtp if cb == 0 else None, inc="st")
        tr_done[b] = P.now("st")
        P.emit("scalar", (lambda b=b, bank=bank: (lambda e: e.activation(
            out=s_lnfm[:, b * 1024:(b + 1) * 1024], in_=PSB[bank][:, 0:1024], func=AF.Copy)))(),
            waits={"st": P.now("st")}, inc="ss")
        lnfm_evac[b] = P.now("ss")

    def part_val(b):
        for eh in range(2):
            bank = eh
            wvv = merge_w({"ss": max(lnfm_evac[b], val_evac.get(b - 2, 0)),
                           "dV": vspill.get(b - 2, 0)}, ld_w["W_val"])
            for cb in range(8):
                P.emit("tensor", (lambda b=b, eh=eh, cb=cb, bank=bank: (lambda e: e.matmul(
                    PS[bank][:, :], s_lnfm[:, b * 1024 + cb * 128: b * 1024 + cb * 128 + 128],
                    s_wval[:, cb * 1024 + eh * 512: cb * 1024 + eh * 512 + 512],
                    start=(cb == 0), stop=(cb == 7))))(),
                    waits=wvv if cb == 0 else None, inc="st")
            P.emit("scalar", (lambda b=b, eh=eh, bank=bank: (lambda e: e.activation(
                out=s_vstage[b % 2][:, eh * 512: eh * 512 + 512], in_=PS[bank][:, :], func=AF.Copy)))(),
                waits={"st": P.now("st")}, inc="ss")
        val_evac[b] = P.now("ss")
        P.emit("sync", (lambda b=b: (lambda e: e.dma_start(
            out=d_val.ap()[128 * b:128 * b + 128, :], in_=s_vstage[b % 2][:, :])))(),
            waits={"ss": val_evac[b]}, inc="dV", amt=16)
        vspill[b] = P.now("dV")

    def part_disp(b):
        ev_prev = [0]
        for m in range(2):
            bank = 2
            wvv = merge_w({"ss": max(lnfm_evac[b], dSD_evac.get(b - 1, 0), ev_prev[0])},
                          ld_w["W_disp"])
            for cb in range(8):
                P.emit("tensor", (lambda b=b, m=m, cb=cb, bank=bank: (lambda e: e.matmul(
                    PS[bank][:, 0:128], s_wdisp[:, cb * 256 + m * 128: cb * 256 + m * 128 + 128],
                    s_lnfm[:, b * 1024 + cb * 128: b * 1024 + cb * 128 + 128],
                    start=(cb == 0), stop=(cb == 7))))(),
                    waits=wvv if cb == 0 else None, inc="st")
            P.emit("scalar", (lambda m=m, bank=bank: (lambda e: e.activation(
                out=s_dscr[:, m * 128:m * 128 + 128], in_=PS[bank][:, 0:128], func=AF.Copy)))(),
                waits={"st": P.now("st")}, inc="ss")
            ev_prev[0] = P.now("ss")
        sv_dscr[b] = P.now("ss")

    def part_dsd(b):
        ev_prev = [0]
        for m in range(2):
            P.emit("tensor", (lambda m=m: (lambda e: e.matmul(
                PS[3][:, 0:128], s_bds[:, :], s_dscr[:, m * 128:m * 128 + 128], start=True, stop=True)))(),
                waits={"ss": max(sv_dscr[b], ev_prev[0])}, inc="st")
            P.emit("tensor", (lambda m=m: (lambda e: e.matmul(
                PS[2][:, 0:128], s_bdd[:, :], s_dscr[:, m * 128:m * 128 + 128], start=True, stop=True)))(),
                inc="st")
            st_d = P.now("st")
            P.emit("scalar", (lambda b=b, m=m: (lambda e: e.activation(
                out=s_dS[:, m * COLS + 128 * b: m * COLS + 128 * b + 128], in_=PS[3][:, 0:128], func=AF.Copy)))(),
                waits={"st": st_d}, inc="ss")
            P.emit("scalar", (lambda b=b, m=m: (lambda e: e.activation(
                out=s_dD[:, m * COLS + 128 * b: m * COLS + 128 * b + 128], in_=PS[2][:, 0:128], func=AF.Copy)))(),
                inc="ss")
            ev_prev[0] = P.now("ss")
        dSD_evac[b] = P.now("ss")

    TPARTS = (part_ln, part_te, part_val, part_disp, part_dsd)
    TKEYS = (0.0, 1.0, 1.5, 2.0, 2.5)

    def emit_tile(b):
        for pf in TPARTS:
            pf(b)


    ctxB = contextlib.ExitStack()
    s_argT = sbuf_in(ctxB, "s_argT", [128, 6144], BF16)
    s_arg = [s_argT[:, 512 * i:512 * i + 512] for i in range(12)]
    s_actT = sbuf_in(ctxB, "s_actT", [128, 6144], BF16)
    s_sig2 = [sbuf_in(ctxB, f"s_sig2_{i}", [128, 512], F32) for i in range(2)]
    s_act = [s_actT[:, 512 * i:512 * i + 512] for i in range(12)]

    s_e = [[sbuf_in(ctxB, f"s_e{h}_{g}", [128, 512], BF16) for g in range(4)] for h in range(2)]
    s_etok0 = sbuf_in(ctxB, "s_etok0", [128, 1024], BF16)
    s_etok = [s_etok0, s_etok0]
    s_eh = sbuf_in(ctxB, "s_eh", [128, 4096], BF16)

    dump_cnt, compact_done = {}, {}
    DE_SEMS = ("de", "de2", "de3", "de4")
    LASTC = len(CHUNKS) - 1
    etok_guard = [0]
    SRC = {("b", 0): (s_dS, 0), ("b", 1): (s_dS, COLS), ("d", 0): (s_dD, 0), ("d", 1): (s_dD, COLS)}
    svG, ssG, stG = {}, {}, {}
    Gc = [0]
    emit_p4a_ref = [None]
    p4a_done_set = set()
    pop_hook = [lambda: None]

    bank_sv = {0: 0, 1: 0}
    bank_ss = {0: 0, 1: 0}

    def emit_chunk(ci, inter=()):
        inter = list(inter)
        G = Gc[0]
        c0, W = CHUNKS[ci]
        nts = W // 128
        tb_hi = min((c0 + W - 1) // 128, NTB - 1)

        def emit_comb(ht, g, k, stv):
            par = k % 2
            P.emit("scalar", (lambda par=par, W=W: (lambda e: e.activation(
                out=s_sig2[par][:, 0:W], in_=PS[6 + par][:, 0:W], func=AF.Tanh,
                scale=0.5, bias=s_bouth[:, 0:1])))(),
                waits={"st": stv, "sv": bank_sv[par]}, inc="ss")
            bank_ss[par] = P.now("ss")
            P.emit("vector", (lambda par=par, W=W: (lambda e: e.scalar_tensor_tensor(
                out=s_sig2[par][:, 0:W], in0=s_sig2[par][:, 0:W], scalar=-5.0,
                in1=PS[4 + par][:, 0:W],
                op0=ALU.mult, op1=ALU.add)))(),
                waits={"ss": P.now("ss"), "st": stv}, inc="sv")
            lgd = P.now("sv")
            bank_sv[par] = lgd
            P.emit("scalar", (lambda ht=ht, g=g, par=par, W=W: (lambda e: e.activation(
                out=s_e[ht][g][:, 0:W], in_=s_sig2[par][:, 0:W], func=AF.Exp,
                bias=s_cst[:, 1:2])))(),
                waits={"sv": lgd}, inc="ss")
            if ci == 0:
                P.emit("vector", (lambda ht=ht, g=g: (lambda e: e.tensor_mul(
                    s_e[ht][g][:, 128:191], s_e[ht][g][:, 128:191],
                    s_emask[:, g * 63:g * 63 + 63])))(),
                    waits={"ss": P.now("ss"), "ld": ld_small}, inc="sv")

        pend = []
        for k, (ht, g) in enumerate([(ht, g) for ht in range(2) for g in range(4)]):
            if len(pend) >= 2:
                emit_comb(*pend.pop(0))
                pop_hook[0]()
                if inter:
                    inter.pop(0)()
            for pi, path in enumerate(("b", "d")):
                srcT, srcOff = SRC[(path, ht)]
                bank = (4 + (k % 2)) if path == "b" else (6 + (k % 2))
                for jgl in range(4):
                    j0 = 16 * g + 4 * jgl
                    q = jgl
                    sbase = 4 * (G % 3)
                    wv = {}
                    if G - 3 in ssG:
                        wv["ss"] = ssG[G - 3]
                    if jgl == 0 and pi == 0 and k == 0:
                        wv["ss"] = max(wv.get("ss", 0), dSD_evac[min(tb_hi, NTB - 1)])
                    for v in range(4):
                        j = j0 + v
                        slot = sbase + v
                        P.emit("vector", (lambda srcT=srcT, srcOff=srcOff, c0=c0, W=W, j=j, slot=slot:
                                          (lambda e: e.tensor_sub(
                                              s_arg[slot][:, 0:W],
                                              srcT[:, srcOff + c0 - 63 + j: srcOff + c0 - 63 + j + W],
                                              srcT[:, srcOff + c0: srcOff + c0 + W])))(),
                               waits=wv if v == 0 else None, inc="sv" if v == 3 else None)
                    svG[G] = P.now("sv")
                    ws = {"sv": svG[G]}
                    if G - 3 in stG:
                        ws["st"] = stG[G - 3]
                    if path == "d" and W == 512:
                        half = G % 3
                        P.emit("scalar", (lambda half=half: (lambda e: e.activation(
                            out=s_actT[:, 2048 * half:2048 * half + 2048],
                            in_=s_argT[:, 2048 * half:2048 * half + 2048], func=AF.Gelu,
                            bias=s_bdmgc[:, 0:1])))(),
                            waits=ws, inc="ss")
                    else:
                        for v in range(4):
                            j = j0 + v
                            slot = sbase + v
                            bias = s_pcols[:, j:j + 1] if path == "b" else s_bdmgc[:, 0:1]
                            P.emit("scalar", (lambda slot=slot, W=W, bias=bias: (lambda e: e.activation(
                                out=s_act[slot][:, 0:W], in_=s_arg[slot][:, 0:W], func=AF.Gelu,
                                bias=bias)))(),
                                waits=ws if v == 0 else None, inc="ss" if v == 3 else None)
                    ssG[G] = P.now("ss")
                    wt = {"ss": ssG[G]}
                    if jgl == 0 and pi == 0:
                        wt["sv"] = max(bank_sv[k % 2], etok_guard[0] if k < 2 else 0)
                    if jgl == 0 and pi == 1:
                        wt["ss"] = max(ssG[G], bank_ss[k % 2])
                    for v in range(4):
                        slot = sbase + v
                        P.emit("tensor", (lambda bank=bank, q=q, v=v, slot=slot, W=W, pi=pi:
                                          (lambda e: e.matmul(
                                              PS[bank][32 * q:32 * q + 32, 0:W],
                                              s_bd32[:, pi * 128 + v * 32: pi * 128 + v * 32 + 32],
                                              s_act[slot][:, 0:W],
                                              start=(v == 0), stop=(v == 3),
                                              tile_position=(0, 32 * q))))(),
                               waits=wt if v == 0 else None, inc="st" if v == 3 else None)
                    stG[G] = P.now("st")
                    G += 1
                    Gc[0] = G
                pop_hook[0]()
            pend.append((ht, g, k, P.now("st")))
        for item in pend:
            emit_comb(*item)
            pop_hook[0]()
            if inter:
                inter.pop(0)()
        comb_sv, comb_ss = P.now("sv"), P.now("ss")
        compact_ts = {}
        css = {}
        for ts in range(nts):
            bank = 4 + ts % 2
            wtt = {"ss": comb_ss, "sv": max(comb_sv, compact_ts.get(ts - 2, 0))}
            for ht in range(2):
                for g in range(4):
                    P.emit("tensor", (lambda bank=bank, ht=ht, g=g, ts=ts: (lambda e: e.transpose(
                        PSB[bank][:, (ht * 4 + g) * 128:(ht * 4 + g) * 128 + 128],
                        s_e[ht][g][:, ts * 128:ts * 128 + 128], s_ident[:, :])))(),
                        waits=wtt if (ht == 0 and g == 0) else None, inc="st")
            P.emit("vector", (lambda ts=ts, bank=bank: (lambda e: e.tensor_copy(
                s_etok[ts % 2][:, :], PSB[bank][:, 0:1024])))(),
                waits={"st": P.now("st")}, inc="sv")
            sv_etok = P.now("sv")
            etok_guard[0] = sv_etok
            gtb = c0 // 128 + ts
            for hG in range(NH):
                ht, hh = hG // 8, hG % 8
                srcap = AP(s_etok[ts % 2][:, :].tensor, ht * 512 + hh, [[1024, 128], [128, 4], [8, 16]])
                zc = (gtb - 1) * 16 + hG
                wvv = {"sv": sv_etok}
                if ci > 0 and ts == 0:
                    wvv = merge_w(wvv, dump_cnt[ci - 1])
                P.emit("vector", (lambda srcap=srcap, hG=hG, ts=ts, zc=zc: (lambda e: e.tensor_scalar(
                    out=s_eh[:, hG * 256 + ts * 64: hG * 256 + ts * 64 + 64], in0=srcap,
                    scalar1=1.0, scalar2=None, op0=ALU.mult, op1=ALU.add,
                    accum_out=s_zsum[:, zc:zc + 1])))(),
                    waits=wvv if hG == 0 else None, inc="sv")
            css[ts] = P.now("sv")
            P.emit("vector", (lambda gtb=gtb: (lambda e: e.reciprocal(
                s_zrec[:, (gtb - 1) * 16:(gtb - 1) * 16 + 16], s_zsum[:, (gtb - 1) * 16:(gtb - 1) * 16 + 16])))(),
                inc="sv")
            compact_ts[ts] = P.now("sv")
        compact_done[ci] = P.now("sv")
        cd_ss = P.now("sv")
        for hG in range(NH):
            off = hG * EB_H + (c0 - 127) * EB_PITCH
            dstp = AP(d_eb.ap().tensor, off, [[EB_PITCH, 128], [128 * EB_PITCH, nts], [1, 64]])
            srcp = s_eh[:, hG * 256: hG * 256 + nts * 64].rearrange("p (t j) -> p t j", j=64)
            P.emit("sync", (lambda dstp=dstp, srcp=srcp: (lambda e: e.dma_start(out=dstp, in_=srcp)))(),
                   waits={"sv": cd_ss, "dz": dz_all} if hG == 0 else None,
                   inc=DE_SEMS[hG % 4], amt=16)
        dump_cnt[ci] = {s: P.now(s) for s in DE_SEMS}
        for f in inter:
            f()

    for _, b, i in sorted((b + TKEYS[i], b, i) for b in range(1, 5) for i in range(5)):
        TPARTS[i](b)
    PQ = [(b, i) for b in range(5, NTB) for i in range(5)]
    PQc = [0]

    def pop_part():
        if PQc[0] < len(PQ):
            b, i = PQ[PQc[0]]
            PQc[0] += 1
            TPARTS[i](b)
    pop_hook[0] = pop_part
    emit_chunk(0)
    emit_chunk(1)
    emit_chunk(2)
    while PQc[0] < len(PQ):
        pop_part()
    pop_hook[0] = lambda: None
    p2_sv, p2_ss, p2_st = P.now("sv"), P.now("ss"), P.now("st")
    dst, src = wslices(s_wvT, dw["W_vT"], 8)
    P.emit("gpsimd", (lambda dst=dst, src=src: (lambda e: e.dma_start(out=dst, in_=src)))(),
           waits={"st": p2_st, "sv": p2_sv}, inc="ld", amt=16)
    ld_wvT = P.now("ld")
    ctxA.close()

    ctxC = contextlib.ExitStack()
    s_wbt = sbuf_in(ctxC, "s_wbt", [128, 4096], BF16, side="right")
    s_wb0 = sbuf_in(ctxC, "s_wb0", [128, 4096], BF16, side="right")
    s_wb = [s_wb0, s_wb0]
    s_atok0 = sbuf_in(ctxC, "s_atok0", [128, 1024], BF16, side="right")
    s_atok = [s_atok0, s_atok0]
    s_afm0 = s_atok0
    s_afm = [s_afm0, s_afm0]
    s_xrel0 = sbuf_in(ctxC, "s_xrel0", [128, C], F32, side="right")
    s_xrel = [s_xrel0, s_xrel0]
    s_valr = sbuf_in(ctxC, "s_valr", [128, 2048], BF16, side="right")

    P.emit("vector", lambda e: e.memset(s_wbt[:, 0:1], 0.0),
           waits={"ss": P.now("ss"), "st": P.now("st"), "sv": P.now("sv"),
                  "dV": vspill[NTB - 1], "dx": P.now("dx"), "dxb": P.now("dxb")}, inc="sv")
    for h in range(NH):
        P.emit("vector", (lambda h=h: (lambda e: e.memset(s_wbt[:, h * 256:h * 256 + 66], 0.0)))(), inc="sv")
    P.emit("vector", lambda e: e.memset(PS[0][:, :], 0.0), inc="sv")
    P.emit("vector", lambda e: e.memset(PS[1][:, :], 0.0), inc="sv")
    P.emit("vector", lambda e: e.memset(s_valr[:, 0:1024], 0.0), inc="sv")
    for cb in range(8):
        P.emit("vector", (lambda cb=cb: (lambda e: e.memset(s_lnfm[:, cb * 2304:cb * 2304 + 256], 0.0)))(), inc="sv")
    p4_init = P.now("sv")

    def chunk_of_col(c):
        for i, (cc0, WW) in enumerate(CHUNKS):
            if cc0 <= c < cc0 + WW:
                return i
        return LASTC

    rb_cnt, wbt_consumed, evacA, afm_evac, yfm_evac, spill_cnt, x2_ready = {}, {}, {}, {}, {}, {}, {}
    wb_evac_b, ytok_used = {}, {}

    DR_SEMS = ("dr", "dr2", "dr3", "dr4")

    def emit_readback(b):
        ci_hi = chunk_of_col(min(128 * b + 127, COLS - 1))
        wvr = merge_w({"dz": dz_all, "sv": p4_init,
                       "st": wbt_consumed.get(b - 1, 0)}, dump_cnt[ci_hi])
        off = (128 * (b - 1) + 1) * EB_PITCH
        rb = {}
        for hg in range(4):
            srcap = AP(d_eb.ap().tensor, off + hg * 4 * EB_H, [[191, 128], [EB_H, 4], [1, 191]])
            dstap = AP(s_wbt[:, :].tensor, 65 + hg * 1024, [[4096, 128], [256, 4], [1, 191]])
            P.emit("sync", (lambda dstap=dstap, srcap=srcap: (lambda e: e.dma_start(out=dstap, in_=srcap)))(),
                   waits=wvr if hg == 0 else None, inc=DR_SEMS[hg], amt=16)
            rb[hg] = (DR_SEMS[hg], P.now(DR_SEMS[hg]))
        rb_cnt[b] = rb

    emit_readback(1)
    vload = {}

    def emit_vload(b, wait_st):
        P.emit("sync", (lambda b=b: (lambda e: e.dma_start(
            out=s_valr[:, (b % 2) * 1024:(b % 2) * 1024 + 1024],
            in_=d_val.ap()[128 * b:128 * b + 128, :])))(),
            waits={"dV": vspill[b], "st": wait_st, "sv": p4_init}, inc="dV2", amt=16)
        vload[b] = P.now("dV2")

    emit_vload(1, 0)
    st_apply_b, dxr_ab = {}, {}

    def pa1(b):
        p4a_done_set.add(b)
        ring = b % 2
        hg_evac = {}
        for hg in range(4):
            bank = hg % 2
            wt = {rb_cnt[b][hg][0]: rb_cnt[b][hg][1], "sv": p4_init,
                  "ss": max(hg_evac.get(hg - 2, 0), wb_evac_b.get(b - 1, 0),
                            afm_evac.get(b - 1, 0))}
            for hh in range(4):
                h = hg * 4 + hh
                P.emit("tensor", (lambda bank=bank, h=h, hh=hh: (lambda e: e.transpose(
                    PSB[bank][:, hh * 256:hh * 256 + 128],
                    s_wbt[:, h * 256 + 128:h * 256 + 256], s_ident[:, :])))(),
                    waits=wt if hh == 0 else None, inc="st")
                P.emit("tensor", (lambda bank=bank, h=h, hh=hh: (lambda e: e.transpose(
                    PSB[bank][:, hh * 256 + 128:hh * 256 + 256],
                    s_wbt[:, h * 256:h * 256 + 128], s_ident[:, :])))(), inc="st")
            P.emit("scalar", (lambda ring=ring, hg=hg, bank=bank: (lambda e: e.activation(
                out=s_wb[ring][:, hg * 1024:hg * 1024 + 1024], in_=PSB[bank][:, 0:1024], func=AF.Copy)))(),
                waits={"st": P.now("st")}, inc="ss")
            hg_evac[hg] = P.now("ss")
        wbt_consumed[b] = P.now("st")
        wb_evac_b[b] = P.now("ss")
        if b + 1 < NTB:
            emit_readback(b + 1)

    def pa2(b):
        ring = b % 2
        pbank = 2
        wt = {"sv": max(evacA.get(b - 1, 0), x2_ready.get(b - 1, 0)),
              "ss": wb_evac_b[b], "dV2": vload[b]}
        for h in range(NH):
            eh = h // 8
            P.emit("tensor", (lambda ring=ring, b=b, h=h, eh=eh, pbank=pbank: (lambda e: e.matmul(
                PS[pbank + eh][:, (h % 8) * 64:(h % 8) * 64 + 64],
                s_wb[ring][:, h * 256:h * 256 + 128],
                s_valr[:, (b % 2) * 1024 + h * 64: (b % 2) * 1024 + h * 64 + 64],
                start=True, stop=False)))(),
                waits=wt if h == 0 else None, inc="st")
            P.emit("tensor", (lambda ring=ring, b=b, h=h, eh=eh, pbank=pbank: (lambda e: e.matmul(
                PS[pbank + eh][:, (h % 8) * 64:(h % 8) * 64 + 64],
                s_wb[ring][:, h * 256 + 128:h * 256 + 256],
                s_valr[:, ((b - 1) % 2) * 1024 + h * 64: ((b - 1) % 2) * 1024 + h * 64 + 64],
                start=False, stop=True)))(), inc="st")
        st_apply = P.now("st")
        st_apply_b[b] = st_apply
        if b + 1 < NTB:
            emit_vload(b + 1, st_apply)
        for eh in range(2):
            zap = AP(s_zrec[:, :].tensor, (b - 1) * 16 + eh * 8, [[272, 128], [1, 8], [0, 64]])
            P.emit("vector", (lambda ring=ring, eh=eh, pbank=pbank, zap=zap: (lambda e: e.tensor_tensor(
                s_atok[ring][:, eh * 512:eh * 512 + 512], PS[pbank + eh][:, :], zap, ALU.mult)))(),
                waits={"st": st_apply, "sv": compact_done[chunk_of_col(128 * b)]} if eh == 0 else None, inc="sv")
        evacA[b] = P.now("sv")

    def pa3(b):
        ring = b % 2
        wt = {"sv": evacA[b], "ss": wb_evac_b[b]}
        for heb in range(8):
            P.emit("tensor", (lambda ring=ring, heb=heb: (lambda e: e.transpose(
                PSB[0][:, heb * 128:heb * 128 + 128], s_atok[ring][:, heb * 128:heb * 128 + 128],
                s_ident[:, :])))(),
                waits=wt if heb == 0 else None, inc="st")
        P.emit("scalar", (lambda ring=ring: (lambda e: e.activation(
            out=s_afm[ring][:, :], in_=PSB[0][:, 0:1024], func=AF.Copy)))(),
            waits={"st": P.now("st")}, inc="ss")
        afm_evac[b] = P.now("ss")
        wvx = {"sv": max(x2_ready.get(b - 1, 0), p4_init), "ds": spill_cnt.get(b - 1, 0)}
        if b == 1:
            P.emit("vector", (lambda ring=ring: (lambda e: e.memset(s_xrel[ring][:, :], 0.0)))(),
                   waits=wvx, inc="sv")
            P.emit("sync", (lambda ring=ring: (lambda e: e.dma_start(out=s_xrel[ring][64:128, :],
                                                                     in_=d_xs[0:64, :])))(),
                   waits={"sv": P.now("sv")}, inc="dxr", amt=16)
        else:
            r0 = 128 * (b - 2) + 64
            P.emit("sync", (lambda ring=ring, r0=r0: (lambda e: e.dma_start(out=s_xrel[ring][0:64, :],
                                                                            in_=d_xs[r0:r0 + 64, :])))(),
                   waits=wvx, inc="dxr", amt=16)
            P.emit("sync", (lambda ring=ring, r0=r0: (lambda e: e.dma_start(out=s_xrel[ring][64:128, :],
                                                                            in_=d_xs[r0 + 64:r0 + 128, :])))(),
                   waits=wvx, inc="dxrb", amt=16)
        dxr_ab[b] = {"dxr": P.now("dxr"), "dxrb": P.now("dxrb")}

    def pa4(b):
        ring = b % 2
        for half in range(2):
            wt = merge_w({"sv": max(evacA[b], x2_ready.get(b - 1, 0)), "ss": afm_evac[b]},
                         ld_w["W_cproj"])
            for heb in range(8):
                P.emit("tensor", (lambda ring=ring, heb=heb, half=half: (lambda e: e.matmul(
                    PS[2 + half][:, :], s_afm[ring][:, heb * 128:heb * 128 + 128],
                    s_wcp[:, heb * 1024 + half * 512: heb * 1024 + half * 512 + 512],
                    start=(heb == 0), stop=(heb == 7))))(),
                    waits=wt if heb == 0 else None, inc="st")
            P.emit("vector", (lambda ring=ring, half=half: (lambda e: e.tensor_tensor(
                s_xrel[ring][:, half * 512:half * 512 + 512], PS[2 + half][:, :],
                s_xrel[ring][:, half * 512:half * 512 + 512], ALU.add)))(),
                waits=merge_w({"st": P.now("st"), "ss": P.now("ss"),
                               "ds": spill_cnt.get(b - 1, 0)}, dxr_ab[b]), inc="sv")
        sv_x2 = P.now("sv")
        x2_ready[b] = sv_x2
        P.emit("sync", (lambda ring=ring, b=b: (lambda e: e.dma_start(
            out=d_x2.ap()[128 * (b - 1):128 * (b - 1) + 128, :], in_=s_xrel[ring][:, :])))(),
            waits={"sv": sv_x2}, inc="ds", amt=16)
        spill_cnt[b] = P.now("ds")

    PAPARTS = (pa1, pa2, pa3, pa4)

    def emit_p4a(b):
        for pf in PAPARTS:
            pf(b)

    P4Q = [(b, i) for b in range(1, 13) for i in range(4)]
    P4Qc = [0]

    def pop_pa():
        if P4Qc[0] < len(P4Q):
            b, i = P4Q[P4Qc[0]]
            P4Qc[0] += 1
            PAPARTS[i](b)
    pop_hook[0] = pop_pa
    emit_chunk(3)
    emit_chunk(4)
    while P4Qc[0] < len(P4Q):
        pop_pa()
    pop_hook[0] = lambda: None
    p3_de = {s: P.now(s) for s in DE_SEMS}
    p3_sv, p3_ss, p3_st = P.now("sv"), P.now("ss"), P.now("st")
    ctxB.close()
    ln2_done = {}
    p4b_done = set()

    def emit_p4b_pair(b0, b1=None):
        bs = [b for b in (b0, b1) if b is not None and b not in p4b_done]
        if not bs:
            return
        dx2v = {}
        for b in bs:
            p4b_done.add(b)
            wvx = {"ds": spill_cnt[b], "sv": ln2_done.get(b - 2, 0), "ss": p2_ss}
            P.emit("gpsimd", (lambda b=b: (lambda e: e.dma_start(
                out=s_x2[b % 2][:, :], in_=d_x2.ap()[128 * (b - 1):128 * (b - 1) + 128, :])))(),
                waits=wvx, inc="dx2", amt=16)
            dx2v[b] = P.now("dx2")
        for b in bs:
            L = 12 * (b % 2)
            xti = s_x2[b % 2]
            P.emit("vector", (lambda xti=xti, L=L: (lambda e: e.reduce_sum(
                s_lncol[:, L:L + 1], xti[:, :], axis=AXX)))(),
                waits={"dx2": dx2v[b]}, inc="sv")
            for h in range(2):
                P.emit("vector", (lambda xti=xti, h=h: (lambda e: e.tensor_mul(
                    PS[7][:, 0:512], xti[:, h * 512:h * 512 + 512], xti[:, h * 512:h * 512 + 512])))(),
                    inc="sv")
                P.emit("vector", (lambda L=L, h=h: (lambda e: e.reduce_sum(
                    s_lncol[:, L + 8 + h:L + 9 + h], PS[7][:, 0:512], axis=AXX)))(), inc="sv")
            P.emit("vector", (lambda L=L: (lambda e: e.scalar_tensor_tensor(
                out=s_lncol[:, L + 1:L + 2], in0=s_lncol[:, L + 8:L + 9], scalar=1.0,
                in1=s_lncol[:, L + 9:L + 10], op0=ALU.mult, op1=ALU.add)))(), inc="sv")
            P.emit("vector", (lambda L=L: (lambda e: e.tensor_scalar(
                out=s_lncol[:, L + 2:L + 3], in0=s_lncol[:, L:L + 1],
                scalar1=C1, scalar2=None, op0=ALU.mult)))(), inc="sv")
            P.emit("vector", (lambda L=L: (lambda e: e.tensor_scalar(
                out=s_lncol[:, L + 3:L + 4], in0=s_lncol[:, L + 1:L + 2],
                scalar1=C1, scalar2=None, op0=ALU.mult)))(), inc="sv")
            P.emit("vector", (lambda L=L: (lambda e: e.tensor_mul(
                s_lncol[:, L + 4:L + 5], s_lncol[:, L + 2:L + 3], s_lncol[:, L + 2:L + 3])))(), inc="sv")
            P.emit("vector", (lambda L=L: (lambda e: e.tensor_sub(
                s_lncol[:, L + 5:L + 6], s_lncol[:, L + 3:L + 4], s_lncol[:, L + 4:L + 5])))(), inc="sv")
        Lmin = 12 * min(b % 2 for b in bs)
        nb = len(bs)
        sq_in = AP(s_lncol[:, :].tensor, Lmin + 5, [[24, 128], [12, nb]])
        sq_out = AP(s_lncol[:, :].tensor, Lmin + 6, [[24, 128], [12, nb]])
        rc_out = AP(s_lncol[:, :].tensor, Lmin + 7, [[24, 128], [12, nb]])
        P.emit("scalar", (lambda sq_in=sq_in, sq_out=sq_out: (lambda e: e.activation(
            out=sq_out, in_=sq_in, func=AF.Sqrt, bias=s_cst[:, 0:1])))(),
            waits={"sv": P.now("sv")}, inc="ss")
        P.emit("vector", (lambda sq_out=sq_out, rc_out=rc_out: (lambda e: e.reciprocal(rc_out, sq_out)))(),
               waits={"ss": P.now("ss")}, inc="sv")
        for b in bs:
            L = 12 * (b % 2)
            P.emit("vector", (lambda b=b, L=L: (lambda e: e.tensor_scalar(
                out=s_ytok2[b % 2][:, :], in0=s_x2[b % 2][:, :],
                scalar1=s_lncol[:, L + 2:L + 3], scalar2=s_lncol[:, L + 7:L + 8],
                op0=ALU.subtract, op1=ALU.mult)))(),
                waits={"st": ytok_used.get(b - 1, 0)}, inc="sv")
            ln2_done[b] = P.now("sv")
            wt = {"sv": ln2_done[b], "ss": max(p3_ss, yfm_evac.get(b - 1, 0))}
            for cb in range(8):
                P.emit("tensor", (lambda cb=cb, b=b: (lambda e: e.transpose(
                    PSB[6][:, cb * 128:cb * 128 + 128], s_ytok2[b % 2][:, cb * 128:cb * 128 + 128],
                    s_ident[:, :])))(),
                    waits=wt if cb == 0 else None, inc="st")
            ytok_used[b] = P.now("st")
            ydst = AP(s_lnfm[:, :].tensor, 128 * b, [[NTB * 1024, 128], [2304, 8], [1, 128]])
            P.emit("scalar", (lambda ydst=ydst: (lambda e: e.activation(
                out=ydst, in_=PSB[6][:, 0:1024], func=AF.Copy)))(),
                waits={"st": P.now("st")}, inc="ss")
            yfm_evac[b] = P.now("ss")

    tailb = [b for b in range(1, NTB) if b not in p4a_done_set]
    order = sorted(((ti + (0, 0.5, 0.9, 1.4)[i], ti, i) for ti in range(len(tailb)) for i in range(4)))
    pairs = [(1, 2), (3, 4), (5, 6), (7, 8), (9, None)]
    oi = 0
    for k in range(5):
        n = 4 if k < 4 else len(order) - oi
        for _ in range(n):
            _, ti, i = order[oi]
            oi += 1
            PAPARTS[i](tailb[ti])
        emit_p4b_pair(*pairs[k])
    p4a_sv, p4a_ss, p4a_st = P.now("sv"), P.now("ss"), P.now("st")
    p4_dr = {s: P.now(s) for s in DR_SEMS}
    p4_ds = P.now("ds")

    ctxC.close()

    ctxD = contextlib.ExitStack()
    s_xr2 = [sbuf_in(ctxD, f"s_xr{i}", [128, 4096], BF16) for i in range(2)]
    s_xk2 = [sbuf_in(ctxD, f"s_xk{i}", [128, 4096], BF16) for i in range(2)]
    s_kact2 = [sbuf_in(ctxD, f"s_kact{i}", [128, 4096], BF16) for i in range(2)]
    s_rtok4 = sbuf_in(ctxD, "s_rtok4", [128, 4096], BF16)
    s_tmp2 = sbuf_in(ctxD, "s_tmp2", [128, 512], BF16)
    s_ffn = sbuf_in(ctxD, "s_ffn", [128, 512], BF16)
    s_kr = sbuf_in(ctxD, "s_kr", [128, 512], F32)
    s_x2r0 = sbuf_in(ctxD, "s_x2r0", [128, C], F32)
    s_x2r = [s_x2r0, s_x2r0]

    kact_sv = {}
    rs4_ss = {}
    kv_v = {}
    sv_mix_f, ss_kact_f, sv_kact_f = {}, {}, {}
    k_used, r_used, kv_used = {}, {}, {}
    first_guard = merge_w(merge_w({"ss": p4a_ss, "ds": p4_ds,
                                   "sv": p4a_sv, "st": p4a_st}, p3_de), p4_dr)

    def emit_mix(fc):
        gc0 = 256 + 512 * fc
        b_hi = (gc0 + 511) // 128
        fp = fc % 2
        wv = {"ss": yfm_evac[b_hi], "ld": ld_wvT,
              "st": max(k_used.get(fc - 2, 0), r_used.get(fc - 2, 0))}
        if fc < 2:
            wv = merge_w(wv, first_guard)
        for cb in range(8):
            P.emit("vector", (lambda cb=cb, gc0=gc0: (lambda e: e.tensor_sub(
                s_tmp2[:, 0:512], s_lnfm[:, cb * 2304 + gc0: cb * 2304 + gc0 + 512],
                s_lnfm[:, cb * 2304 + gc0 - 1: cb * 2304 + gc0 + 511])))(),
                waits=wv if cb == 0 else None, inc="sv")
            P.emit("vector", (lambda cb=cb, gc0=gc0, fp=fp: (lambda e: e.scalar_tensor_tensor(
                out=s_xr2[fp][:, cb * 512:cb * 512 + 512], in0=s_tmp2[:, 0:512], scalar=s_mur[:, cb:cb + 1],
                in1=s_lnfm[:, cb * 2304 + gc0 - 1: cb * 2304 + gc0 + 511],
                op0=ALU.mult, op1=ALU.add)))(), inc="sv")
            P.emit("vector", (lambda cb=cb, gc0=gc0, fp=fp: (lambda e: e.scalar_tensor_tensor(
                out=s_xk2[fp][:, cb * 512:cb * 512 + 512], in0=s_tmp2[:, 0:512], scalar=s_muk[:, cb:cb + 1],
                in1=s_lnfm[:, cb * 2304 + gc0 - 1: cb * 2304 + gc0 + 511],
                op0=ALU.mult, op1=ALU.add)))(), inc="sv")
        sv_mix_f[fc] = P.now("sv")

    def emit_kph(fc):
        fp = fc % 2
        for fb in range(8):
            bank = 2 + fb % 2
            wt = merge_w({"sv": sv_mix_f[fc],
                          "ss": kact_sv.get((fc, fb - 2), kact_sv.get((fc - 1, fb + 6), 0))},
                         ld_w["W_kT"])
            for cb in range(8):
                P.emit("tensor", (lambda fb=fb, cb=cb, bank=bank, fp=fp: (lambda e: e.matmul(
                    PS[bank][:, :], s_wkT[:, cb * 1024 + fb * 128: cb * 1024 + fb * 128 + 128],
                    s_xk2[fp][:, cb * 512: cb * 512 + 512],
                    start=(cb == 0), stop=(cb == 7))))(),
                    waits=wt if cb == 0 else None, inc="st")
            P.emit("scalar", (lambda bank=bank: (lambda e: e.activation(
                out=s_kr[:, :], in_=PS[bank][:, :], func=AF.Relu)))(),
                waits={"st": P.now("st"), "ss": P.now("ss")}, inc="ss")
            P.emit("scalar", (lambda fb=fb, fp=fp: (lambda e: e.activation(
                out=s_kact2[fp][:, fb * 512:fb * 512 + 512], in_=s_kr[:, :], func=AF.Square)))(),
                waits={"st": kv_used.get(fc - 2, 0)} if fb == 0 else None, inc="ss")
            kact_sv[(fc, fb)] = P.now("ss")
        ss_kact_f[fc] = P.now("ss")
        sv_kact_f[fc] = P.now("sv")
        k_used[fc] = P.now("st")

    def emit_rph(fc):
        fp = fc % 2
        for tl in range(4):
            for eh in range(2):
                bank = 2 * (tl % 2) + eh
                wt = merge_w({"sv": sv_mix_f[fc],
                              "ss": max(rs4_ss.get((fc, tl - 2), rs4_ss.get((fc - 1, tl + 2), 0)),
                                        ss_kact_f[fc] if tl % 2 == 1 else 0)},
                             ld_w["W_rT"])
                for cb in range(8):
                    P.emit("tensor", (lambda tl=tl, eh=eh, cb=cb, bank=bank, fp=fp: (lambda e: e.matmul(
                        PS[bank][:, :], s_xr2[fp][:, cb * 512 + tl * 128: cb * 512 + tl * 128 + 128],
                        s_wrT[:, cb * 1024 + eh * 512: cb * 1024 + eh * 512 + 512],
                        start=(cb == 0), stop=(cb == 7))))(),
                        waits=wt if cb == 0 else None, inc="st")
                P.emit("scalar", (lambda tl=tl, eh=eh, bank=bank: (lambda e: e.activation(
                    out=s_rtok4[:, tl * 1024 + eh * 512: tl * 1024 + eh * 512 + 512],
                    in_=PS[bank][:, :], func=AF.Sigmoid)))(),
                    waits={"st": P.now("st"), "sv": kv_v.get((fc - 1, tl), 0)}, inc="ss")
            rs4_ss[(fc, tl)] = P.now("ss")
        r_used[fc] = P.now("st")

    def emit_kvph(fc):
        fp = fc % 2
        for tl in range(4):
            bglob = 2 + 4 * fc + tl
            ring = tl % 2
            wvx = {"ds": spill_cnt[bglob - 1], "sv": P.now("sv"),
                   "do_": P.now("do_")}
            P.emit("sync", (lambda ring=ring, bglob=bglob: (lambda e: e.dma_start(
                out=s_x2r[ring][:, :], in_=d_x2.ap()[128 * (bglob - 1):128 * (bglob - 1) + 128, :])))(),
                waits=wvx, inc="dl", amt=16)
            dl_t = P.now("dl")
            for eh in range(2):
                bank = 4 + eh
                wt = {"sv": max(sv_kact_f[fc], kv_v.get((fc, tl - 1), kv_v.get((fc - 1, 3), 0))),
                      "ss": max(rs4_ss[(fc, tl)], ss_kact_f[fc]), "ld": ld_wvT}
                for fb in range(8):
                    P.emit("tensor", (lambda tl=tl, eh=eh, fb=fb, bank=bank, fp=fp: (lambda e: e.matmul(
                        PS[bank][:, :], s_kact2[fp][:, fb * 512 + tl * 128: fb * 512 + tl * 128 + 128],
                        s_wvT[:, fb * 1024 + eh * 512: fb * 1024 + eh * 512 + 512],
                        start=(fb == 0), stop=(fb == 7))))(),
                        waits=wt if fb == 0 else None, inc="st")
                P.emit("vector", (lambda tl=tl, eh=eh, bank=bank: (lambda e: e.tensor_mul(
                    s_ffn[:, 0:512], PS[bank][:, :],
                    s_rtok4[:, tl * 1024 + eh * 512: tl * 1024 + eh * 512 + 512])))(),
                    waits={"st": P.now("st"), "ss": rs4_ss[(fc, tl)]}, inc="sv")
                P.emit("vector", (lambda ring=ring, eh=eh: (lambda e: e.tensor_add(
                    s_x2r[ring][:, eh * 512:eh * 512 + 512], s_ffn[:, 0:512],
                    s_x2r[ring][:, eh * 512:eh * 512 + 512])))(),
                    waits={"dl": dl_t, "do_": P.now("do_")}, inc="sv")
            kv_v[(fc, tl)] = P.now("sv")
            P.emit("sync", (lambda ring=ring, bglob=bglob: (lambda e: e.dma_start(
                out=d_out[128 * (bglob - 2):128 * (bglob - 2) + 128, :], in_=s_x2r[ring][:, :])))(),
                waits={"sv": P.now("sv")}, inc="do_", amt=16)
        kv_used[fc] = P.now("st")

    emit_mix(0)
    emit_kph(0)
    emit_p4b_pair(10, 11)
    emit_rph(0)
    emit_mix(1)
    emit_p4b_pair(12, 13)
    emit_kvph(0)
    emit_kph(1)
    emit_p4b_pair(14, 15)
    emit_rph(1)
    emit_mix(2)
    emit_p4b_pair(16, 17)
    emit_kvph(1)
    emit_kph(2)
    emit_rph(2)
    emit_mix(3)
    emit_kvph(2)
    emit_kph(3)
    emit_rph(3)
    emit_kvph(3)
    do_all = P.now("do_")
    P.emit("sync", lambda e: e.wait_ge(sems["do_"], do_all))
    ctxD.close()
    ctxC2.close()

    OWN = {"vector": "sv", "scalar": "ss", "tensor": "st"}
    with nc.Block() as block:
        def make_runner(engname):
            def run(eng):
                lastw = {}
                own = OWN.get(engname)
                own_cnt = 0
                dma_cnt = {}
                for (e, waits, fn, incn, amt) in P.ops:
                    if e != engname:
                        continue
                    if own is not None and own_cnt > lastw.get(own, 0) and waits.get(own, 0) < own_cnt:
                        waits = dict(waits)
                        waits[own] = own_cnt
                    if own is None and incn is not None:
                        prev = dma_cnt.get(incn, 0)
                        if prev > waits.get(incn, 0):
                            waits = dict(waits)
                            waits[incn] = prev
                        dma_cnt[incn] = prev + amt
                    for s, v in waits.items():
                        if v > lastw.get(s, 0):
                            eng.wait_ge(sems[s], v)
                            lastw[s] = v
                    inst = fn(eng)
                    if incn is not None and inst is not None:
                        inst.then_inc(sems[incn], amt)
                        if incn == own:
                            own_cnt += amt
            return run
        block.vector(make_runner("vector"))
        block.scalar(make_runner("scalar"))
        block.tensor(make_runner("tensor"))
        block.sync(make_runner("sync"))
        block.gpsimd(make_runner("gpsimd"))

    ctx.close()
    return nc


def _host_consts(inp):
    c = {}
    bds = np.zeros((128, 128), np.float32)
    bdd = np.zeros((128, 128), np.float32)
    for h in range(8):
        bds[16 * h:16 * h + 16, 16 * h:16 * h + 16] = inp["L0_attn_W_strain"]
        bdd[16 * h:16 * h + 16, 16 * h:16 * h + 16] = inp["L0_attn_W_dmg"]
    c["BDs"], c["BDd"] = bds, bdd
    bd32 = np.zeros((128, 256), np.float32)
    for pi, W in enumerate((inp["L0_attn_W_bond"], inp["L0_attn_W_dmg_out"])):
        for v in range(4):
            for h in range(8):
                bd32[h * 16:h * 16 + 16, pi * 128 + v * 32 + v * 8 + h] = W[:, 0]
    c["BD32"] = bd32
    c["IDENT"] = np.eye(128, dtype=np.float32)
    c["relT"] = np.ascontiguousarray(inp["L0_attn_rel_pos_emb"][:64].T)
    c["W_pos"] = inp["L0_attn_W_pos"]
    c["b_dmgC"] = np.ascontiguousarray(np.tile(inp["L0_attn_b_dmg"].reshape(16, 1), (8, 1)))
    c["b_outC"] = np.ascontiguousarray(np.broadcast_to(inp["L0_attn_b_dmg_out"].reshape(1, 1), (128, 1)).copy())
    rep = np.zeros((16, 128), np.float32)
    for d in range(16):
        rep[d, d::16] = 1.0
    c["REP16"] = rep
    c["mu_r"] = np.ascontiguousarray(inp["L0_ffn_mu_r"].reshape(8, 128).T)
    c["mu_k"] = np.ascontiguousarray(inp["L0_ffn_mu_k"].reshape(8, 128).T)
    c["W_disp"] = inp["L0_attn_W_disp"]
    c["W_val"] = inp["L0_attn_W_val"]
    c["W_cproj"] = inp["L0_attn_W_cproj"]
    c["W_rT"] = np.ascontiguousarray(inp["L0_ffn_W_r"].T)
    c["W_kT"] = np.ascontiguousarray(inp["L0_ffn_W_k"].T)
    c["W_vT"] = np.ascontiguousarray(inp["L0_ffn_W_v"].T)
    return c


def kernel(**inputs):
    inputs = {k: np.ascontiguousarray(np.asarray(v), dtype=np.float32) for k, v in inputs.items()}
    x = inputs["L0_attn_x"]
    consts = _host_consts(inputs)
    mask_edge = np.zeros((128, 252), np.float32)
    for g in range(4):
        for js in range(16):
            for h in range(8):
                p = 8 * js + h
                j = 16 * g + js
                for trel in range(63):
                    mask_edge[p, g * 63 + trel] = 1.0 if j >= 63 - trel else 0.0
    mask_ones = np.ones((128, 252), np.float32)

    in_maps = []
    for core in range(NCORES):
        bb, s0 = core // 4, (core % 4) * 2048
        xs = np.zeros((2112, C), np.float32)
        xs[64:] = x[bb, s0:s0 + 2048]
        if s0 > 0:
            xs[:64] = x[bb, s0 - 64:s0]
        m = dict(consts)
        m["xs"] = xs
        m["EMASK"] = mask_edge if s0 == 0 else mask_ones
        in_maps.append(m)

    nc = build_nc()
    res = run_bass_kernel_spmd(nc, in_maps, core_ids=list(range(NCORES)))
    out = np.zeros((B, T, C), np.float32)
    for core in range(NCORES):
        bb, s0 = core // 4, (core % 4) * 2048
        out[bb, s0:s0 + 2048] = res.results[core]["out"]
    return out


# version 26
# speedup vs baseline: 1.0360x; 1.0179x over previous
import contextlib
import numpy as np

import concourse.bass as bass
import concourse.mybir as mybir
from concourse.ap import AP
from concourse.bass_utils import run_bass_kernel_spmd

F32 = mybir.dt.float32
BF16 = mybir.dt.bfloat16
AF = mybir.ActivationFunctionType
ALU = mybir.AluOpType
AXX = mybir.AxisListType.X

NCORES = 8
B, T, C = 2, 8192, 1024
NH, WIN = 16, 64
TOUT = 2048
COLS = 2304
NTB = 18
EB_PITCH = 192
EB_ROWS = 2177
EB_H = EB_ROWS * EB_PITCH
CHUNKS = [(128, 512), (640, 512), (1152, 512), (1664, 512), (2176, 128)]
C1 = 1.0 / C


class Plan:
    def __init__(self):
        self.ops = []
        self.cnt = {}

    def emit(self, eng, fn, waits=None, inc=None, amt=1):
        w = {k: v for k, v in (waits or {}).items() if v}
        self.ops.append([eng, w, fn, inc, amt])
        if inc is not None:
            self.cnt[inc] = self.cnt.get(inc, 0) + amt
        return self.cnt.get(inc, 0)

    def add_wait_last(self, sem, val):
        if val:
            w = self.ops[-1][1]
            w[sem] = max(w.get(sem, 0), val)

    def now(self, sem):
        return self.cnt.get(sem, 0)


def build_nc():
    nc = bass.Bass(trn_type="TRN2", target_bir_lowering=False, debug=False)

    d_xs = nc.dram_tensor("xs", [2112, C], F32, kind="ExternalInput").ap()
    dw = {}
    for nm, shp in [
        ("W_disp", [C, 256]), ("W_val", [C, C]), ("W_cproj", [C, C]),
        ("W_rT", [C, C]), ("W_kT", [C, C]), ("W_vT", [C, C]),
        ("BDs", [128, 128]), ("BDd", [128, 128]), ("BD32", [128, 256]),
        ("IDENT", [128, 128]), ("EMASK", [128, 252]),
        ("relT", [16, 64]), ("W_pos", [16, 16]),
        ("b_dmgC", [128, 1]), ("b_outC", [128, 1]), ("REP16", [16, 128]),
        ("mu_r", [128, 8]), ("mu_k", [128, 8]),
    ]:
        dw[nm] = nc.dram_tensor(nm, shp, F32, kind="ExternalInput").ap()
    d_out = nc.dram_tensor("out", [TOUT, C], F32, kind="ExternalOutput").ap()
    d_eb = nc.dram_tensor("Ebuf", [NH * EB_H], BF16, kind="Internal")
    d_x2 = nc.dram_tensor("X2buf", [2176, C], F32, kind="Internal")
    d_val = nc.dram_tensor("VALbuf", [COLS, C], BF16, kind="Internal")

    ctx = contextlib.ExitStack()
    def sbuf_in(cx, n, s, d, side=None):
        return cx.enter_context(nc.sbuf_tensor(n, s, d, side=side) if side else nc.sbuf_tensor(n, s, d))
    sbuf = lambda n, s, d: sbuf_in(ctx, n, s, d)
    psumt = lambda n: ctx.enter_context(nc.psum_tensor(n, [128, 512], F32))
    mksem = lambda n: ctx.enter_context(nc.semaphore(n))

    s_wdisp = sbuf("s_wdisp", [128, 2048], BF16)
    s_wval = sbuf("s_wval", [128, 8192], BF16)
    s_wcp = sbuf("s_wcp", [128, 8192], BF16)
    s_wrT = sbuf("s_wrT", [128, 8192], BF16)
    s_wkT = sbuf("s_wkT", [128, 8192], BF16)
    s_wvT = s_wval

    s_bds = sbuf("s_bds", [128, 128], BF16)
    s_bdd = sbuf("s_bdd", [128, 128], BF16)
    s_bd32 = sbuf("s_bd32", [128, 256], BF16)
    s_ident = sbuf("s_ident", [128, 128], BF16)
    s_emask = sbuf("s_emask", [128, 252], BF16)
    s_pcols = sbuf("s_pcols", [128, 64], F32)
    s_bdmgc = sbuf("s_bdmgc", [128, 1], F32)
    s_bout = sbuf("s_bout", [128, 1], F32)
    s_bouth = sbuf("s_bouth", [128, 1], F32)
    s_mur = sbuf("s_mur", [128, 8], F32)
    s_muk = sbuf("s_muk", [128, 8], F32)

    s_lnfm = sbuf("s_lnfm", [128, NTB * 1024], BF16)
    s_dS = sbuf("s_dS", [128, 2 * COLS], BF16)
    s_dD = sbuf("s_dD", [128, 2 * COLS], BF16)
    s_zrec = sbuf("s_zrec", [128, 17 * 16], F32)
    s_zsum = s_zrec
    s_lncol = sbuf("s_lncol", [128, 24], F32)
    s_cst = sbuf("s_cst", [128, 2], F32)
    ctx0 = contextlib.ExitStack()
    s_relT = sbuf_in(ctx0, "s_relT", [16, 64], F32)
    s_wpos = sbuf_in(ctx0, "s_wpos", [16, 16], F32)
    s_rep = sbuf_in(ctx0, "s_rep", [16, 128], F32)
    s_pfm = sbuf_in(ctx0, "s_pfm", [16, 64], F32)
    s_zero = sbuf_in(ctx0, "s_zero", [128, 2178], BF16)

    PS = [psumt(f"ps{i}") for i in range(8)]
    PSB = [PS[i][:, :].bitcast(BF16) for i in range(8)]

    sems = {n: mksem(n) for n in
            ["ld", "ld2", "ld3", "ld4", "ld5", "dx", "dxb", "dz", "de", "de2", "de3", "de4",
             "dr", "dr2", "dr3", "dr4", "dxr", "dxrb", "dx2", "ds", "dl", "do_",
             "dV", "dV2", "sv", "ss", "st"]}
    P = Plan()

    def wslices(sb_t, dram, nb):
        Fd = dram.shape[1]
        src = AP(dram.tensor, 0, [[Fd, 128], [128 * Fd, nb], [1, Fd]])
        dst = sb_t[:, :].rearrange("p (b f) -> p b f", b=nb)
        return dst, src

    for sb_t, nm in [(s_bds, "BDs"), (s_bdd, "BDd"), (s_bd32, "BD32"),
                     (s_ident, "IDENT"), (s_emask, "EMASK")]:
        P.emit("gpsimd", (lambda d=sb_t, nm=nm: (lambda e: e.dma_start(out=d[:, :], in_=dw[nm][:, :])))(),
               inc="ld", amt=16)
    ld_small = P.now("ld")
    ld_w = {}
    for (sb_t, nm), sm in zip([(s_wdisp, "W_disp"), (s_wval, "W_val"), (s_wcp, "W_cproj"),
                               (s_wrT, "W_rT"), (s_wkT, "W_kT")],
                              ("ld", "ld2", "ld3", "ld4", "ld5")):
        dst, src2 = wslices(sb_t, dw[nm], 8)
        P.emit("gpsimd", (lambda dst=dst, src2=src2: (lambda e: e.dma_start(out=dst, in_=src2)))(),
               inc=sm, amt=16)
        ld_w[nm] = {sm: P.now(sm)}
    for sb_t, nm in [(s_relT, "relT"), (s_wpos, "W_pos"), (s_bdmgc, "b_dmgC"),
                     (s_bout, "b_outC"), (s_rep, "REP16"),
                     (s_mur, "mu_r"), (s_muk, "mu_k")]:
        P.emit("sync", (lambda d=sb_t, nm=nm: (lambda e: e.dma_start(out=d[:, :], in_=dw[nm][:, :])))(),
               inc="dx", amt=16)
    ld_all = P.now("ld")
    dx_consts = P.now("dx")

    P.emit("vector", lambda e: e.memset(s_zero[:, :], 0.0), inc="sv")
    P.emit("vector", lambda e: e.memset(s_cst[:, 0:1], 1e-5), inc="sv")
    P.emit("vector", lambda e: e.memset(s_cst[:, 1:2], -5.0), inc="sv")
    P.emit("vector", lambda e: e.memset(s_lnfm[:, 0:1024], 0.0), inc="sv")
    for m in range(2):
        P.emit("vector", (lambda m=m: (lambda e: e.memset(s_dS[:, m * COLS:m * COLS + 128], 0.0)))(), inc="sv")
        P.emit("vector", (lambda m=m: (lambda e: e.memset(s_dD[:, m * COLS:m * COLS + 128], 0.0)))(), inc="sv")
    sv_memsets = P.now("sv")

    P.emit("tensor", lambda e: e.matmul(PS[0][0:16, 0:64], s_wpos[:, :], s_relT[:, :], start=True, stop=True),
           waits={"dx": dx_consts}, inc="st")
    P.emit("vector", lambda e: e.tensor_copy(s_pfm[:, :], PS[0][0:16, 0:64]),
           waits={"st": P.now("st")}, inc="sv")
    P.emit("tensor", lambda e: e.matmul(PS[0][0:128, 64:128], s_rep[:, :], s_pfm[:, :], start=True, stop=True),
           waits={"sv": P.now("sv")}, inc="st")
    P.emit("vector", lambda e: e.tensor_copy(s_pcols[:, :], PS[0][0:128, 64:128]),
           waits={"st": P.now("st")}, inc="sv")
    P.emit("vector", lambda e: e.tensor_scalar(out=s_bouth[:, :], in0=s_bout[:, :],
                                               scalar1=0.5, scalar2=None, op0=ALU.mult),
           waits={"dx": dx_consts}, inc="sv")
    sv_p0 = P.now("sv")

    def emit_ln(xti, lnt, wv, ws, apply_extra=None, sqout=None, sq2=False):
        P.emit("vector", (lambda xti=xti: (lambda e: e.reduce_sum(s_lncol[:, 0:1], xti[:, :], axis=AXX)))(),
               waits=wv, inc="sv")
        if sq2:
            for h in range(2):
                P.emit("vector", (lambda xti=xti, h=h: (lambda e: e.tensor_mul(
                    PS[7][:, 0:512], xti[:, h * 512:h * 512 + 512], xti[:, h * 512:h * 512 + 512])))(),
                    waits=ws if h == 0 else None, inc="sv")
                P.emit("vector", (lambda h=h: (lambda e: e.reduce_sum(
                    s_lncol[:, 8 + h:9 + h], PS[7][:, 0:512], axis=AXX)))(), inc="sv")
            P.emit("vector", lambda e: e.scalar_tensor_tensor(
                out=s_lncol[:, 1:2], in0=s_lncol[:, 8:9], scalar=1.0,
                in1=s_lncol[:, 9:10], op0=ALU.mult, op1=ALU.add), inc="sv")
            P.emit("vector", lambda e: e.tensor_scalar(out=s_lncol[:, 2:3], in0=s_lncol[:, 0:1],
                                                       scalar1=C1, scalar2=None, op0=ALU.mult), inc="sv")
        else:
            P.emit("scalar", (lambda xti=xti, sqout=sqout: (lambda e: e.activation(out=sqout[:, :], in_=xti[:, :], func=AF.Square,
                                                                      accum_out=s_lncol[:, 1:2])))(),
                   waits=ws, inc="ss")
            P.emit("scalar", lambda e: e.activation(out=s_lncol[:, 2:3], in_=s_lncol[:, 0:1], func=AF.Copy, scale=C1),
                   waits={"sv": P.now("sv")}, inc="ss")
        P.emit("vector", lambda e: e.tensor_scalar(out=s_lncol[:, 3:4], in0=s_lncol[:, 1:2],
                                                   scalar1=C1, scalar2=None, op0=ALU.mult),
               waits=None if sq2 else {"ss": P.now("ss")}, inc="sv")
        P.emit("vector", lambda e: e.tensor_mul(s_lncol[:, 4:5], s_lncol[:, 2:3], s_lncol[:, 2:3]), inc="sv")
        P.emit("vector", lambda e: e.tensor_sub(s_lncol[:, 5:6], s_lncol[:, 3:4], s_lncol[:, 4:5]), inc="sv")
        P.emit("scalar", lambda e: e.activation(out=s_lncol[:, 6:7], in_=s_lncol[:, 5:6], func=AF.Sqrt, bias=s_cst[:, 0:1]),
               waits={"sv": P.now("sv")}, inc="ss")
        P.emit("vector", lambda e: e.reciprocal(s_lncol[:, 7:8], s_lncol[:, 6:7]),
               waits={"ss": P.now("ss")}, inc="sv")
        P.emit("vector", (lambda xti=xti, lnt=lnt: (lambda e: e.tensor_scalar(
            out=lnt[:, :], in0=xti[:, :], scalar1=s_lncol[:, 2:3], scalar2=s_lncol[:, 7:8],
            op0=ALU.subtract, op1=ALU.mult)))(),
            waits=merge_w({}, apply_extra or {}), inc="sv")
        return P.now("sv")

    ctx0.close()
    ctxC2 = contextlib.ExitStack()
    s_x2 = [sbuf_in(ctxC2, f"s_x2{i}", [128, C], F32, side="right") for i in range(2)]
    s_ytok0 = sbuf_in(ctxC2, "s_ytok0", [128, C], BF16, side="right")
    s_ytok2 = [s_ytok0, s_ytok0]
    s_ytok = s_ytok0
    ctxA = contextlib.ExitStack()
    s_xt = [sbuf_in(ctxA, f"s_xt{i}", [128, C], F32, side="right") for i in range(2)]
    s_lntok = [sbuf_in(ctxA, f"s_lntok{i}", [128, C], BF16, side="right") for i in range(2)]
    s_dscr = sbuf_in(ctxA, "s_dscr", [128, 256], BF16, side="right")
    s_vstage = [sbuf_in(ctxA, f"s_vstage{i}", [128, 1024], BF16, side="right") for i in range(2)]

    P.emit("vector", lambda e: e.memset(s_xt[1][0:64, :], 0.0), inc="sv")
    P.emit("vector", lambda e: e.memset(s_vstage[0][:, :], 0.0), inc="sv")
    sv_x1ms = P.now("sv")
    P.emit("sync", lambda e: e.dma_start(out=d_val.ap()[0:128, :], in_=s_vstage[0][:, :]),
           waits={"sv": sv_x1ms}, inc="dV", amt=16)
    vspill = {0: P.now("dV")}
    P.emit("sync", lambda e: e.dma_start(out=s_xt[1][64:128, :], in_=d_xs[0:64, :]),
           inc="dx", amt=16)
    xdma = {1: {"dx": P.now("dx")}}

    DZ_SEMS = ("dz", "de", "de2", "de3")
    z1h = [s_zero[:, 0:1024], s_zero[:, 1024:2176]]
    for h in range(NH):
        off = h * EB_H
        P.emit("sync", (lambda off=off: (lambda e: e.dma_start(
            out=AP(d_eb.ap().tensor, off, [[1, 192]]), in_=s_zero[0:1, 0:192])))(),
            waits={"sv": sv_memsets} if h == 0 else None, inc=DZ_SEMS[h % 4], amt=16)
        for hf in range(2):
            d2 = AP(d_eb.ap().tensor, off + 192 + 64 + hf * 1024 * 192, [[192, 1024 + hf * 128], [1, 128]])
            P.emit("sync", (lambda d2=d2, hf=hf: (lambda e: e.dma_start(out=d2, in_=z1h[hf])))(),
                   inc=DZ_SEMS[(h + 1 + hf) % 4], amt=16)
    dz_all = {s: P.now(s) for s in DZ_SEMS}

    def merge_w(base, extra):
        w = dict(base)
        for k, v in extra.items():
            w[k] = max(w.get(k, 0), v)
        return w

    lnfm_evac, val_evac, dSD_evac, ln_done, tr_done = {}, {}, {}, {0: sv_p0}, {}
    sv_dscr = {}

    def part_ln(b):
        xti, lnt = s_xt[b % 2], s_lntok[b % 2]
        if b >= 2:
            r0 = 128 * (b - 2) + 64
            wl = {"sv": ln_done.get(b - 2, 0)}
            P.emit("sync", (lambda xti=xti, r0=r0: (lambda e: e.dma_start(
                out=xti[0:64, :], in_=d_xs[r0:r0 + 64, :])))(),
                waits=wl, inc="dx", amt=16)
            P.emit("sync", (lambda xti=xti, r0=r0: (lambda e: e.dma_start(
                out=xti[64:128, :], in_=d_xs[r0 + 64:r0 + 128, :])))(),
                waits=wl, inc="dxb", amt=16)
            xdma[b] = {"dx": P.now("dx"), "dxb": P.now("dxb")}
        wv = merge_w({"sv": max(sv_memsets, sv_x1ms), "ss": P.now("ss")}, xdma[b])
        ws = merge_w({"sv": max(sv_memsets, sv_x1ms)}, xdma[b])
        ap_extra = {"st": tr_done.get(b - 2, 0)}
        ln_done[b] = emit_ln(xti, lnt, wv, ws, ap_extra, sqout=s_x2[0])

    def part_te(b):
        lnt = s_lntok[b % 2]
        bank = b % 2
        wtp = {"sv": ln_done[b], "ss": max(lnfm_evac.get(b - 2, 0), val_evac.get(b - 1, 0)),
               "ld": ld_small, "dx": dx_consts}
        for cb in range(8):
            P.emit("tensor", (lambda lnt=lnt, bank=bank, cb=cb: (lambda e: e.transpose(
                PSB[bank][:, cb * 128:cb * 128 + 128], lnt[:, cb * 128:cb * 128 + 128], s_ident[:, :])))(),
                waits=wtp if cb == 0 else None, inc="st")
        tr_done[b] = P.now("st")
        P.emit("scalar", (lambda b=b, bank=bank: (lambda e: e.activation(
            out=s_lnfm[:, b * 1024:(b + 1) * 1024], in_=PSB[bank][:, 0:1024], func=AF.Copy)))(),
            waits={"st": P.now("st")}, inc="ss")
        lnfm_evac[b] = P.now("ss")

    def part_val(b):
        for eh in range(2):
            bank = eh
            wvv = merge_w({"ss": max(lnfm_evac[b], val_evac.get(b - 2, 0)),
                           "dV": vspill.get(b - 2, 0)}, ld_w["W_val"])
            for cb in range(8):
                P.emit("tensor", (lambda b=b, eh=eh, cb=cb, bank=bank: (lambda e: e.matmul(
                    PS[bank][:, :], s_lnfm[:, b * 1024 + cb * 128: b * 1024 + cb * 128 + 128],
                    s_wval[:, cb * 1024 + eh * 512: cb * 1024 + eh * 512 + 512],
                    start=(cb == 0), stop=(cb == 7))))(),
                    waits=wvv if cb == 0 else None, inc="st")
            P.emit("scalar", (lambda b=b, eh=eh, bank=bank: (lambda e: e.activation(
                out=s_vstage[b % 2][:, eh * 512: eh * 512 + 512], in_=PS[bank][:, :], func=AF.Copy)))(),
                waits={"st": P.now("st")}, inc="ss")
        val_evac[b] = P.now("ss")
        P.emit("sync", (lambda b=b: (lambda e: e.dma_start(
            out=d_val.ap()[128 * b:128 * b + 128, :], in_=s_vstage[b % 2][:, :])))(),
            waits={"ss": val_evac[b]}, inc="dV", amt=16)
        vspill[b] = P.now("dV")

    def part_disp(b):
        ev_prev = [0]
        for m in range(2):
            bank = 2
            wvv = merge_w({"ss": max(lnfm_evac[b], dSD_evac.get(b - 1, 0), ev_prev[0])},
                          ld_w["W_disp"])
            for cb in range(8):
                P.emit("tensor", (lambda b=b, m=m, cb=cb, bank=bank: (lambda e: e.matmul(
                    PS[bank][:, 0:128], s_wdisp[:, cb * 256 + m * 128: cb * 256 + m * 128 + 128],
                    s_lnfm[:, b * 1024 + cb * 128: b * 1024 + cb * 128 + 128],
                    start=(cb == 0), stop=(cb == 7))))(),
                    waits=wvv if cb == 0 else None, inc="st")
            P.emit("scalar", (lambda m=m, bank=bank: (lambda e: e.activation(
                out=s_dscr[:, m * 128:m * 128 + 128], in_=PS[bank][:, 0:128], func=AF.Copy)))(),
                waits={"st": P.now("st")}, inc="ss")
            ev_prev[0] = P.now("ss")
        sv_dscr[b] = P.now("ss")

    def part_dsd(b):
        ev_prev = [0]
        for m in range(2):
            P.emit("tensor", (lambda m=m: (lambda e: e.matmul(
                PS[3][:, 0:128], s_bds[:, :], s_dscr[:, m * 128:m * 128 + 128], start=True, stop=True)))(),
                waits={"ss": max(sv_dscr[b], ev_prev[0])}, inc="st")
            P.emit("tensor", (lambda m=m: (lambda e: e.matmul(
                PS[2][:, 0:128], s_bdd[:, :], s_dscr[:, m * 128:m * 128 + 128], start=True, stop=True)))(),
                inc="st")
            st_d = P.now("st")
            P.emit("scalar", (lambda b=b, m=m: (lambda e: e.activation(
                out=s_dS[:, m * COLS + 128 * b: m * COLS + 128 * b + 128], in_=PS[3][:, 0:128], func=AF.Copy)))(),
                waits={"st": st_d}, inc="ss")
            P.emit("scalar", (lambda b=b, m=m: (lambda e: e.activation(
                out=s_dD[:, m * COLS + 128 * b: m * COLS + 128 * b + 128], in_=PS[2][:, 0:128], func=AF.Copy)))(),
                inc="ss")
            ev_prev[0] = P.now("ss")
        dSD_evac[b] = P.now("ss")

    TPARTS = (part_ln, part_te, part_val, part_disp, part_dsd)
    TKEYS = (0.0, 1.0, 1.5, 2.0, 2.5)

    def emit_tile(b):
        for pf in TPARTS:
            pf(b)


    ctxB = contextlib.ExitStack()
    s_argT = sbuf_in(ctxB, "s_argT", [128, 6144], BF16)
    s_arg = [s_argT[:, 512 * i:512 * i + 512] for i in range(12)]
    s_actT = sbuf_in(ctxB, "s_actT", [128, 6144], BF16)
    s_sig2 = [sbuf_in(ctxB, f"s_sig2_{i}", [128, 512], F32) for i in range(2)]
    s_act = [s_actT[:, 512 * i:512 * i + 512] for i in range(12)]

    s_e = [[sbuf_in(ctxB, f"s_e{h}_{g}", [128, 512], BF16) for g in range(4)] for h in range(2)]
    s_etok0 = sbuf_in(ctxB, "s_etok0", [128, 1024], BF16)
    s_etok = [s_etok0, s_etok0]
    s_eh = sbuf_in(ctxB, "s_eh", [128, 4096], BF16)

    dump_cnt, compact_done = {}, {}
    DE_SEMS = ("de", "de2", "de3", "de4")
    LASTC = len(CHUNKS) - 1
    etok_guard = [0]
    SRC = {("b", 0): (s_dS, 0), ("b", 1): (s_dS, COLS), ("d", 0): (s_dD, 0), ("d", 1): (s_dD, COLS)}
    svG, ssG, stG = {}, {}, {}
    Gc = [0]
    emit_p4a_ref = [None]
    p4a_done_set = set()
    pop_hook = [lambda: None]

    bank_sv = {0: 0, 1: 0}
    bank_ss = {0: 0, 1: 0}

    def emit_chunk(ci, inter=()):
        inter = list(inter)
        G = Gc[0]
        c0, W = CHUNKS[ci]
        nts = W // 128
        tb_hi = min((c0 + W - 1) // 128, NTB - 1)

        def emit_comb(ht, g, k, stv):
            par = k % 2
            P.emit("scalar", (lambda par=par, W=W: (lambda e: e.activation(
                out=s_sig2[par][:, 0:W], in_=PS[6 + par][:, 0:W], func=AF.Tanh,
                scale=0.5, bias=s_bouth[:, 0:1])))(),
                waits={"st": stv, "sv": bank_sv[par]}, inc="ss")
            bank_ss[par] = P.now("ss")
            P.emit("vector", (lambda par=par, W=W: (lambda e: e.scalar_tensor_tensor(
                out=s_sig2[par][:, 0:W], in0=s_sig2[par][:, 0:W], scalar=-5.0,
                in1=PS[4 + par][:, 0:W],
                op0=ALU.mult, op1=ALU.add)))(),
                waits={"ss": P.now("ss"), "st": stv}, inc="sv")
            lgd = P.now("sv")
            bank_sv[par] = lgd
            P.emit("scalar", (lambda ht=ht, g=g, par=par, W=W: (lambda e: e.activation(
                out=s_e[ht][g][:, 0:W], in_=s_sig2[par][:, 0:W], func=AF.Exp,
                bias=s_cst[:, 1:2])))(),
                waits={"sv": lgd}, inc="ss")
            if ci == 0:
                P.emit("vector", (lambda ht=ht, g=g: (lambda e: e.tensor_mul(
                    s_e[ht][g][:, 128:191], s_e[ht][g][:, 128:191],
                    s_emask[:, g * 63:g * 63 + 63])))(),
                    waits={"ss": P.now("ss"), "ld": ld_small}, inc="sv")

        pend = []
        for k, (ht, g) in enumerate([(ht, g) for ht in range(2) for g in range(4)]):
            if len(pend) >= 2:
                emit_comb(*pend.pop(0))
                pop_hook[0]()
                if inter:
                    inter.pop(0)()
            for pi, path in enumerate(("b", "d")):
                srcT, srcOff = SRC[(path, ht)]
                bank = (4 + (k % 2)) if path == "b" else (6 + (k % 2))
                for jgl in range(4):
                    j0 = 16 * g + 4 * jgl
                    q = jgl
                    sbase = 4 * (G % 3)
                    wv = {}
                    if G - 3 in ssG:
                        wv["ss"] = ssG[G - 3]
                    if jgl == 0 and pi == 0 and k == 0:
                        wv["ss"] = max(wv.get("ss", 0), dSD_evac[min(tb_hi, NTB - 1)])
                        if ci == 0:
                            wv = merge_w(wv, dz_all)
                    for v in range(4):
                        j = j0 + v
                        slot = sbase + v
                        P.emit("vector", (lambda srcT=srcT, srcOff=srcOff, c0=c0, W=W, j=j, slot=slot:
                                          (lambda e: e.tensor_sub(
                                              s_arg[slot][:, 0:W],
                                              srcT[:, srcOff + c0 - 63 + j: srcOff + c0 - 63 + j + W],
                                              srcT[:, srcOff + c0: srcOff + c0 + W])))(),
                               waits=wv if v == 0 else None, inc="sv" if v == 3 else None)
                    svG[G] = P.now("sv")
                    ws = {"sv": svG[G]}
                    if G - 3 in stG:
                        ws["st"] = stG[G - 3]
                    if path == "d" and W == 512:
                        half = G % 3
                        P.emit("scalar", (lambda half=half: (lambda e: e.activation(
                            out=s_actT[:, 2048 * half:2048 * half + 2048],
                            in_=s_argT[:, 2048 * half:2048 * half + 2048], func=AF.Gelu,
                            bias=s_bdmgc[:, 0:1])))(),
                            waits=ws, inc="ss")
                    else:
                        for v in range(4):
                            j = j0 + v
                            slot = sbase + v
                            bias = s_pcols[:, j:j + 1] if path == "b" else s_bdmgc[:, 0:1]
                            P.emit("scalar", (lambda slot=slot, W=W, bias=bias: (lambda e: e.activation(
                                out=s_act[slot][:, 0:W], in_=s_arg[slot][:, 0:W], func=AF.Gelu,
                                bias=bias)))(),
                                waits=ws if v == 0 else None, inc="ss" if v == 3 else None)
                    ssG[G] = P.now("ss")
                    wt = {"ss": ssG[G]}
                    if jgl == 0 and pi == 0:
                        wt["sv"] = max(bank_sv[k % 2], etok_guard[0] if k < 2 else 0)
                    if jgl == 0 and pi == 1:
                        wt["ss"] = max(ssG[G], bank_ss[k % 2])
                    for v in range(4):
                        slot = sbase + v
                        P.emit("tensor", (lambda bank=bank, q=q, v=v, slot=slot, W=W, pi=pi:
                                          (lambda e: e.matmul(
                                              PS[bank][32 * q:32 * q + 32, 0:W],
                                              s_bd32[:, pi * 128 + v * 32: pi * 128 + v * 32 + 32],
                                              s_act[slot][:, 0:W],
                                              start=(v == 0), stop=(v == 3),
                                              tile_position=(0, 32 * q))))(),
                               waits=wt if v == 0 else None, inc="st" if v == 3 else None)
                    stG[G] = P.now("st")
                    G += 1
                    Gc[0] = G
                pop_hook[0]()
            pend.append((ht, g, k, P.now("st")))
        for item in pend:
            emit_comb(*item)
            pop_hook[0]()
            if inter:
                inter.pop(0)()
        comb_sv, comb_ss = P.now("sv"), P.now("ss")
        compact_ts = {}
        css = {}
        for ts in range(nts):
            bank = 4 + ts % 2
            wtt = {"ss": comb_ss, "sv": max(comb_sv, compact_ts.get(ts - 2, 0))}
            for ht in range(2):
                for g in range(4):
                    P.emit("tensor", (lambda bank=bank, ht=ht, g=g, ts=ts: (lambda e: e.transpose(
                        PSB[bank][:, (ht * 4 + g) * 128:(ht * 4 + g) * 128 + 128],
                        s_e[ht][g][:, ts * 128:ts * 128 + 128], s_ident[:, :])))(),
                        waits=wtt if (ht == 0 and g == 0) else None, inc="st")
            P.emit("vector", (lambda ts=ts, bank=bank: (lambda e: e.tensor_copy(
                s_etok[ts % 2][:, :], PSB[bank][:, 0:1024])))(),
                waits={"st": P.now("st")}, inc="sv")
            sv_etok = P.now("sv")
            etok_guard[0] = sv_etok
            gtb = c0 // 128 + ts
            for hG in range(NH):
                ht, hh = hG // 8, hG % 8
                srcap = AP(s_etok[ts % 2][:, :].tensor, ht * 512 + hh, [[1024, 128], [128, 4], [8, 16]])
                zc = (gtb - 1) * 16 + hG
                wvv = {"sv": sv_etok}
                if ci > 0 and ts == 0:
                    wvv = merge_w(wvv, dump_cnt[ci - 1])
                P.emit("vector", (lambda srcap=srcap, hG=hG, ts=ts, zc=zc: (lambda e: e.tensor_scalar(
                    out=s_eh[:, hG * 256 + ts * 64: hG * 256 + ts * 64 + 64], in0=srcap,
                    scalar1=1.0, scalar2=None, op0=ALU.mult, op1=ALU.add,
                    accum_out=s_zsum[:, zc:zc + 1])))(),
                    waits=wvv if hG == 0 else None, inc="sv")
            css[ts] = P.now("sv")
            P.emit("vector", (lambda gtb=gtb: (lambda e: e.reciprocal(
                s_zrec[:, (gtb - 1) * 16:(gtb - 1) * 16 + 16], s_zsum[:, (gtb - 1) * 16:(gtb - 1) * 16 + 16])))(),
                inc="sv")
            compact_ts[ts] = P.now("sv")
        compact_done[ci] = P.now("sv")
        cd_ss = P.now("sv")
        for hG in range(NH):
            off = hG * EB_H + (c0 - 127) * EB_PITCH
            dstp = AP(d_eb.ap().tensor, off, [[EB_PITCH, 128], [128 * EB_PITCH, nts], [1, 64]])
            srcp = s_eh[:, hG * 256: hG * 256 + nts * 64].rearrange("p (t j) -> p t j", j=64)
            P.emit("sync", (lambda dstp=dstp, srcp=srcp: (lambda e: e.dma_start(out=dstp, in_=srcp)))(),
                   waits=merge_w({"sv": cd_ss}, dz_all) if hG == 0 else None,
                   inc=DE_SEMS[hG % 4], amt=16)
        dump_cnt[ci] = {s: P.now(s) for s in DE_SEMS}
        for f in inter:
            f()

    for _, b, i in sorted((b + TKEYS[i], b, i) for b in range(1, 5) for i in range(5)):
        TPARTS[i](b)
    PQ = [(b, i) for b in range(5, NTB) for i in range(5)]
    PQc = [0]

    def pop_part():
        if PQc[0] < len(PQ):
            b, i = PQ[PQc[0]]
            PQc[0] += 1
            TPARTS[i](b)
    pop_hook[0] = pop_part
    emit_chunk(0)
    emit_chunk(1)
    emit_chunk(2)
    while PQc[0] < len(PQ):
        pop_part()
    pop_hook[0] = lambda: None
    p2_sv, p2_ss, p2_st = P.now("sv"), P.now("ss"), P.now("st")
    dst, src = wslices(s_wvT, dw["W_vT"], 8)
    P.emit("gpsimd", (lambda dst=dst, src=src: (lambda e: e.dma_start(out=dst, in_=src)))(),
           waits={"st": p2_st, "sv": p2_sv}, inc="ld", amt=16)
    ld_wvT = P.now("ld")
    ctxA.close()

    ctxC = contextlib.ExitStack()
    s_wbt = sbuf_in(ctxC, "s_wbt", [128, 4096], BF16, side="right")
    s_wb0 = sbuf_in(ctxC, "s_wb0", [128, 4096], BF16, side="right")
    s_wb = [s_wb0, s_wb0]
    s_atok0 = sbuf_in(ctxC, "s_atok0", [128, 1024], BF16, side="right")
    s_atok = [s_atok0, s_atok0]
    s_afm0 = s_atok0
    s_afm = [s_afm0, s_afm0]
    s_xrel0 = sbuf_in(ctxC, "s_xrel0", [128, C], F32, side="right")
    s_xrel = [s_xrel0, s_xrel0]
    s_valr = sbuf_in(ctxC, "s_valr", [128, 2048], BF16, side="right")

    P.emit("vector", lambda e: e.memset(s_wbt[:, 0:1], 0.0),
           waits={"ss": P.now("ss"), "st": P.now("st"), "sv": P.now("sv"),
                  "dV": vspill[NTB - 1], "dx": P.now("dx"), "dxb": P.now("dxb")}, inc="sv")
    for h in range(NH):
        P.emit("vector", (lambda h=h: (lambda e: e.memset(s_wbt[:, h * 256:h * 256 + 66], 0.0)))(), inc="sv")
    P.emit("vector", lambda e: e.memset(PS[0][:, :], 0.0), inc="sv")
    P.emit("vector", lambda e: e.memset(PS[1][:, :], 0.0), inc="sv")
    P.emit("vector", lambda e: e.memset(s_valr[:, 0:1024], 0.0), inc="sv")
    for cb in range(8):
        P.emit("vector", (lambda cb=cb: (lambda e: e.memset(s_lnfm[:, cb * 2304:cb * 2304 + 256], 0.0)))(), inc="sv")
    p4_init = P.now("sv")

    def chunk_of_col(c):
        for i, (cc0, WW) in enumerate(CHUNKS):
            if cc0 <= c < cc0 + WW:
                return i
        return LASTC

    rb_cnt, wbt_consumed, evacA, afm_evac, yfm_evac, spill_cnt, x2_ready = {}, {}, {}, {}, {}, {}, {}
    wb_evac_b, ytok_used = {}, {}

    DR_SEMS = ("dr", "dr2", "dr3", "dr4")

    def emit_readback(b):
        ci_hi = chunk_of_col(min(128 * b + 127, COLS - 1))
        wvr = merge_w(merge_w({"sv": p4_init,
                       "st": wbt_consumed.get(b - 1, 0)}, dump_cnt[ci_hi]), dz_all)
        off = (128 * (b - 1) + 1) * EB_PITCH
        rb = {}
        for hg in range(4):
            srcap = AP(d_eb.ap().tensor, off + hg * 4 * EB_H, [[191, 128], [EB_H, 4], [1, 191]])
            dstap = AP(s_wbt[:, :].tensor, 65 + hg * 1024, [[4096, 128], [256, 4], [1, 191]])
            P.emit("sync", (lambda dstap=dstap, srcap=srcap: (lambda e: e.dma_start(out=dstap, in_=srcap)))(),
                   waits=wvr if hg == 0 else None, inc=DR_SEMS[hg], amt=16)
            rb[hg] = (DR_SEMS[hg], P.now(DR_SEMS[hg]))
        rb_cnt[b] = rb

    emit_readback(1)
    vload = {}

    def emit_vload(b, wait_st):
        P.emit("sync", (lambda b=b: (lambda e: e.dma_start(
            out=s_valr[:, (b % 2) * 1024:(b % 2) * 1024 + 1024],
            in_=d_val.ap()[128 * b:128 * b + 128, :])))(),
            waits={"dV": vspill[b], "st": wait_st, "sv": p4_init}, inc="dV2", amt=16)
        vload[b] = P.now("dV2")

    emit_vload(1, 0)
    st_apply_b, dxr_ab = {}, {}

    def pa1(b):
        p4a_done_set.add(b)
        ring = b % 2
        hg_evac = {}
        for hg in range(4):
            bank = hg % 2
            wt = {rb_cnt[b][hg][0]: rb_cnt[b][hg][1], "sv": p4_init,
                  "ss": max(hg_evac.get(hg - 2, 0), wb_evac_b.get(b - 1, 0),
                            afm_evac.get(b - 1, 0))}
            for hh in range(4):
                h = hg * 4 + hh
                P.emit("tensor", (lambda bank=bank, h=h, hh=hh: (lambda e: e.transpose(
                    PSB[bank][:, hh * 256:hh * 256 + 128],
                    s_wbt[:, h * 256 + 128:h * 256 + 256], s_ident[:, :])))(),
                    waits=wt if hh == 0 else None, inc="st")
                P.emit("tensor", (lambda bank=bank, h=h, hh=hh: (lambda e: e.transpose(
                    PSB[bank][:, hh * 256 + 128:hh * 256 + 256],
                    s_wbt[:, h * 256:h * 256 + 128], s_ident[:, :])))(), inc="st")
            P.emit("scalar", (lambda ring=ring, hg=hg, bank=bank: (lambda e: e.activation(
                out=s_wb[ring][:, hg * 1024:hg * 1024 + 1024], in_=PSB[bank][:, 0:1024], func=AF.Copy)))(),
                waits={"st": P.now("st")}, inc="ss")
            hg_evac[hg] = P.now("ss")
        wbt_consumed[b] = P.now("st")
        wb_evac_b[b] = P.now("ss")
        if b + 1 < NTB:
            emit_readback(b + 1)

    def pa2(b):
        ring = b % 2
        pbank = 2
        wt = {"sv": max(evacA.get(b - 1, 0), x2_ready.get(b - 1, 0)),
              "ss": wb_evac_b[b], "dV2": vload[b]}
        for h in range(NH):
            eh = h // 8
            P.emit("tensor", (lambda ring=ring, b=b, h=h, eh=eh, pbank=pbank: (lambda e: e.matmul(
                PS[pbank + eh][:, (h % 8) * 64:(h % 8) * 64 + 64],
                s_wb[ring][:, h * 256:h * 256 + 128],
                s_valr[:, (b % 2) * 1024 + h * 64: (b % 2) * 1024 + h * 64 + 64],
                start=True, stop=False)))(),
                waits=wt if h == 0 else None, inc="st")
            P.emit("tensor", (lambda ring=ring, b=b, h=h, eh=eh, pbank=pbank: (lambda e: e.matmul(
                PS[pbank + eh][:, (h % 8) * 64:(h % 8) * 64 + 64],
                s_wb[ring][:, h * 256 + 128:h * 256 + 256],
                s_valr[:, ((b - 1) % 2) * 1024 + h * 64: ((b - 1) % 2) * 1024 + h * 64 + 64],
                start=False, stop=True)))(), inc="st")
        st_apply = P.now("st")
        st_apply_b[b] = st_apply
        if b + 1 < NTB:
            emit_vload(b + 1, st_apply)
        for eh in range(2):
            zap = AP(s_zrec[:, :].tensor, (b - 1) * 16 + eh * 8, [[272, 128], [1, 8], [0, 64]])
            P.emit("vector", (lambda ring=ring, eh=eh, pbank=pbank, zap=zap: (lambda e: e.tensor_tensor(
                s_atok[ring][:, eh * 512:eh * 512 + 512], PS[pbank + eh][:, :], zap, ALU.mult)))(),
                waits={"st": st_apply, "sv": compact_done[chunk_of_col(128 * b)]} if eh == 0 else None, inc="sv")
        evacA[b] = P.now("sv")

    def pa3(b):
        ring = b % 2
        wt = {"sv": evacA[b], "ss": wb_evac_b[b]}
        for heb in range(8):
            P.emit("tensor", (lambda ring=ring, heb=heb: (lambda e: e.transpose(
                PSB[0][:, heb * 128:heb * 128 + 128], s_atok[ring][:, heb * 128:heb * 128 + 128],
                s_ident[:, :])))(),
                waits=wt if heb == 0 else None, inc="st")
        P.emit("scalar", (lambda ring=ring: (lambda e: e.activation(
            out=s_afm[ring][:, :], in_=PSB[0][:, 0:1024], func=AF.Copy)))(),
            waits={"st": P.now("st")}, inc="ss")
        afm_evac[b] = P.now("ss")
        wvx = {"sv": max(x2_ready.get(b - 1, 0), p4_init), "ds": spill_cnt.get(b - 1, 0)}
        if b == 1:
            P.emit("vector", (lambda ring=ring: (lambda e: e.memset(s_xrel[ring][:, :], 0.0)))(),
                   waits=wvx, inc="sv")
            P.emit("sync", (lambda ring=ring: (lambda e: e.dma_start(out=s_xrel[ring][64:128, :],
                                                                     in_=d_xs[0:64, :])))(),
                   waits={"sv": P.now("sv")}, inc="dxr", amt=16)
        else:
            r0 = 128 * (b - 2) + 64
            P.emit("sync", (lambda ring=ring, r0=r0: (lambda e: e.dma_start(out=s_xrel[ring][0:64, :],
                                                                            in_=d_xs[r0:r0 + 64, :])))(),
                   waits=wvx, inc="dxr", amt=16)
            P.emit("sync", (lambda ring=ring, r0=r0: (lambda e: e.dma_start(out=s_xrel[ring][64:128, :],
                                                                            in_=d_xs[r0 + 64:r0 + 128, :])))(),
                   waits=wvx, inc="dxrb", amt=16)
        dxr_ab[b] = {"dxr": P.now("dxr"), "dxrb": P.now("dxrb")}

    def pa4(b):
        ring = b % 2
        for half in range(2):
            wt = merge_w({"sv": max(evacA[b], x2_ready.get(b - 1, 0)), "ss": afm_evac[b]},
                         ld_w["W_cproj"])
            for heb in range(8):
                P.emit("tensor", (lambda ring=ring, heb=heb, half=half: (lambda e: e.matmul(
                    PS[2 + half][:, :], s_afm[ring][:, heb * 128:heb * 128 + 128],
                    s_wcp[:, heb * 1024 + half * 512: heb * 1024 + half * 512 + 512],
                    start=(heb == 0), stop=(heb == 7))))(),
                    waits=wt if heb == 0 else None, inc="st")
            P.emit("vector", (lambda ring=ring, half=half: (lambda e: e.tensor_tensor(
                s_xrel[ring][:, half * 512:half * 512 + 512], PS[2 + half][:, :],
                s_xrel[ring][:, half * 512:half * 512 + 512], ALU.add)))(),
                waits=merge_w({"st": P.now("st"), "ss": P.now("ss"),
                               "ds": spill_cnt.get(b - 1, 0)}, dxr_ab[b]), inc="sv")
        sv_x2 = P.now("sv")
        x2_ready[b] = sv_x2
        P.emit("sync", (lambda ring=ring, b=b: (lambda e: e.dma_start(
            out=d_x2.ap()[128 * (b - 1):128 * (b - 1) + 128, :], in_=s_xrel[ring][:, :])))(),
            waits={"sv": sv_x2}, inc="ds", amt=16)
        spill_cnt[b] = P.now("ds")

    PAPARTS = (pa1, pa2, pa3, pa4)

    def emit_p4a(b):
        for pf in PAPARTS:
            pf(b)

    P4Q = [(b, i) for b in range(1, 13) for i in range(4)]
    P4Qc = [0]

    def pop_pa():
        if P4Qc[0] < len(P4Q):
            b, i = P4Q[P4Qc[0]]
            P4Qc[0] += 1
            PAPARTS[i](b)
    pop_hook[0] = pop_pa
    emit_chunk(3)
    emit_chunk(4)
    while P4Qc[0] < len(P4Q):
        pop_pa()
    pop_hook[0] = lambda: None
    p3_de = {s: P.now(s) for s in DE_SEMS}
    p3_sv, p3_ss, p3_st = P.now("sv"), P.now("ss"), P.now("st")
    ctxB.close()
    ln2_done = {}
    p4b_done = set()

    def emit_p4b_pair(b0, b1=None):
        bs = [b for b in (b0, b1) if b is not None and b not in p4b_done]
        if not bs:
            return
        dx2v = {}
        for b in bs:
            p4b_done.add(b)
            wvx = {"ds": spill_cnt[b], "sv": ln2_done.get(b - 2, 0), "ss": p2_ss}
            P.emit("gpsimd", (lambda b=b: (lambda e: e.dma_start(
                out=s_x2[b % 2][:, :], in_=d_x2.ap()[128 * (b - 1):128 * (b - 1) + 128, :])))(),
                waits=wvx, inc="dx2", amt=16)
            dx2v[b] = P.now("dx2")
        for b in bs:
            L = 12 * (b % 2)
            xti = s_x2[b % 2]
            P.emit("vector", (lambda xti=xti, L=L: (lambda e: e.reduce_sum(
                s_lncol[:, L:L + 1], xti[:, :], axis=AXX)))(),
                waits={"dx2": dx2v[b]}, inc="sv")
            for h in range(2):
                P.emit("vector", (lambda xti=xti, h=h: (lambda e: e.tensor_mul(
                    PS[7][:, 0:512], xti[:, h * 512:h * 512 + 512], xti[:, h * 512:h * 512 + 512])))(),
                    inc="sv")
                P.emit("vector", (lambda L=L, h=h: (lambda e: e.reduce_sum(
                    s_lncol[:, L + 8 + h:L + 9 + h], PS[7][:, 0:512], axis=AXX)))(), inc="sv")
            P.emit("vector", (lambda L=L: (lambda e: e.scalar_tensor_tensor(
                out=s_lncol[:, L + 1:L + 2], in0=s_lncol[:, L + 8:L + 9], scalar=1.0,
                in1=s_lncol[:, L + 9:L + 10], op0=ALU.mult, op1=ALU.add)))(), inc="sv")
            P.emit("vector", (lambda L=L: (lambda e: e.tensor_scalar(
                out=s_lncol[:, L + 2:L + 3], in0=s_lncol[:, L:L + 1],
                scalar1=C1, scalar2=None, op0=ALU.mult)))(), inc="sv")
            P.emit("vector", (lambda L=L: (lambda e: e.tensor_scalar(
                out=s_lncol[:, L + 3:L + 4], in0=s_lncol[:, L + 1:L + 2],
                scalar1=C1, scalar2=None, op0=ALU.mult)))(), inc="sv")
            P.emit("vector", (lambda L=L: (lambda e: e.tensor_mul(
                s_lncol[:, L + 4:L + 5], s_lncol[:, L + 2:L + 3], s_lncol[:, L + 2:L + 3])))(), inc="sv")
            P.emit("vector", (lambda L=L: (lambda e: e.tensor_sub(
                s_lncol[:, L + 5:L + 6], s_lncol[:, L + 3:L + 4], s_lncol[:, L + 4:L + 5])))(), inc="sv")
        Lmin = 12 * min(b % 2 for b in bs)
        nb = len(bs)
        sq_in = AP(s_lncol[:, :].tensor, Lmin + 5, [[24, 128], [12, nb]])
        sq_out = AP(s_lncol[:, :].tensor, Lmin + 6, [[24, 128], [12, nb]])
        rc_out = AP(s_lncol[:, :].tensor, Lmin + 7, [[24, 128], [12, nb]])
        P.emit("scalar", (lambda sq_in=sq_in, sq_out=sq_out: (lambda e: e.activation(
            out=sq_out, in_=sq_in, func=AF.Sqrt, bias=s_cst[:, 0:1])))(),
            waits={"sv": P.now("sv")}, inc="ss")
        P.emit("vector", (lambda sq_out=sq_out, rc_out=rc_out: (lambda e: e.reciprocal(rc_out, sq_out)))(),
               waits={"ss": P.now("ss")}, inc="sv")
        for b in bs:
            L = 12 * (b % 2)
            P.emit("vector", (lambda b=b, L=L: (lambda e: e.tensor_scalar(
                out=s_ytok2[b % 2][:, :], in0=s_x2[b % 2][:, :],
                scalar1=s_lncol[:, L + 2:L + 3], scalar2=s_lncol[:, L + 7:L + 8],
                op0=ALU.subtract, op1=ALU.mult)))(),
                waits={"st": ytok_used.get(b - 1, 0)}, inc="sv")
            ln2_done[b] = P.now("sv")
            wt = {"sv": ln2_done[b], "ss": max(p3_ss, yfm_evac.get(b - 1, 0))}
            for cb in range(8):
                P.emit("tensor", (lambda cb=cb, b=b: (lambda e: e.transpose(
                    PSB[6][:, cb * 128:cb * 128 + 128], s_ytok2[b % 2][:, cb * 128:cb * 128 + 128],
                    s_ident[:, :])))(),
                    waits=wt if cb == 0 else None, inc="st")
            ytok_used[b] = P.now("st")
            ydst = AP(s_lnfm[:, :].tensor, 128 * b, [[NTB * 1024, 128], [2304, 8], [1, 128]])
            P.emit("scalar", (lambda ydst=ydst: (lambda e: e.activation(
                out=ydst, in_=PSB[6][:, 0:1024], func=AF.Copy)))(),
                waits={"st": P.now("st")}, inc="ss")
            yfm_evac[b] = P.now("ss")

    tailb = [b for b in range(1, NTB) if b not in p4a_done_set]
    order = sorted(((ti + (0, 0.5, 0.9, 1.4)[i], ti, i) for ti in range(len(tailb)) for i in range(4)))
    pairs = [(1, 2), (3, 4), (5, 6), (7, 8), (9, None)]
    oi = 0
    for k in range(5):
        n = 4 if k < 4 else len(order) - oi
        for _ in range(n):
            _, ti, i = order[oi]
            oi += 1
            PAPARTS[i](tailb[ti])
        emit_p4b_pair(*pairs[k])
    p4a_sv, p4a_ss, p4a_st = P.now("sv"), P.now("ss"), P.now("st")
    p4_dr = {s: P.now(s) for s in DR_SEMS}
    p4_ds = P.now("ds")

    ctxC.close()

    ctxD = contextlib.ExitStack()
    s_xr2 = [sbuf_in(ctxD, f"s_xr{i}", [128, 4096], BF16) for i in range(2)]
    s_xk2 = [sbuf_in(ctxD, f"s_xk{i}", [128, 4096], BF16) for i in range(2)]
    s_kact2 = [sbuf_in(ctxD, f"s_kact{i}", [128, 4096], BF16) for i in range(2)]
    s_rtok4 = sbuf_in(ctxD, "s_rtok4", [128, 4096], BF16)
    s_tmp2 = sbuf_in(ctxD, "s_tmp2", [128, 512], BF16)
    s_ffn = sbuf_in(ctxD, "s_ffn", [128, 512], BF16)
    s_kr = sbuf_in(ctxD, "s_kr", [128, 512], F32)
    s_x2r0 = sbuf_in(ctxD, "s_x2r0", [128, C], F32)
    s_x2r = [s_x2r0, s_x2r0]

    kact_sv = {}
    rs4_ss = {}
    kv_v = {}
    sv_mix_f, ss_kact_f, sv_kact_f = {}, {}, {}
    k_used, r_used, kv_used = {}, {}, {}
    first_guard = merge_w(merge_w({"ss": p4a_ss, "ds": p4_ds,
                                   "sv": p4a_sv, "st": p4a_st}, p3_de), p4_dr)

    def emit_mix(fc):
        gc0 = 256 + 512 * fc
        b_hi = (gc0 + 511) // 128
        fp = fc % 2
        wv = {"ss": yfm_evac[b_hi], "ld": ld_wvT,
              "st": max(k_used.get(fc - 2, 0), r_used.get(fc - 2, 0))}
        if fc < 2:
            wv = merge_w(wv, first_guard)
        for cb in range(8):
            P.emit("vector", (lambda cb=cb, gc0=gc0: (lambda e: e.tensor_sub(
                s_tmp2[:, 0:512], s_lnfm[:, cb * 2304 + gc0: cb * 2304 + gc0 + 512],
                s_lnfm[:, cb * 2304 + gc0 - 1: cb * 2304 + gc0 + 511])))(),
                waits=wv if cb == 0 else None, inc="sv")
            P.emit("vector", (lambda cb=cb, gc0=gc0, fp=fp: (lambda e: e.scalar_tensor_tensor(
                out=s_xr2[fp][:, cb * 512:cb * 512 + 512], in0=s_tmp2[:, 0:512], scalar=s_mur[:, cb:cb + 1],
                in1=s_lnfm[:, cb * 2304 + gc0 - 1: cb * 2304 + gc0 + 511],
                op0=ALU.mult, op1=ALU.add)))(), inc="sv")
            P.emit("vector", (lambda cb=cb, gc0=gc0, fp=fp: (lambda e: e.scalar_tensor_tensor(
                out=s_xk2[fp][:, cb * 512:cb * 512 + 512], in0=s_tmp2[:, 0:512], scalar=s_muk[:, cb:cb + 1],
                in1=s_lnfm[:, cb * 2304 + gc0 - 1: cb * 2304 + gc0 + 511],
                op0=ALU.mult, op1=ALU.add)))(), inc="sv")
        sv_mix_f[fc] = P.now("sv")

    def emit_kph(fc):
        fp = fc % 2
        for fb in range(8):
            bank = 2 + fb % 2
            wt = merge_w({"sv": sv_mix_f[fc],
                          "ss": kact_sv.get((fc, fb - 2), kact_sv.get((fc - 1, fb + 6), 0))},
                         ld_w["W_kT"])
            for cb in range(8):
                P.emit("tensor", (lambda fb=fb, cb=cb, bank=bank, fp=fp: (lambda e: e.matmul(
                    PS[bank][:, :], s_wkT[:, cb * 1024 + fb * 128: cb * 1024 + fb * 128 + 128],
                    s_xk2[fp][:, cb * 512: cb * 512 + 512],
                    start=(cb == 0), stop=(cb == 7))))(),
                    waits=wt if cb == 0 else None, inc="st")
            P.emit("scalar", (lambda bank=bank: (lambda e: e.activation(
                out=s_kr[:, :], in_=PS[bank][:, :], func=AF.Relu)))(),
                waits={"st": P.now("st"), "ss": P.now("ss")}, inc="ss")
            P.emit("scalar", (lambda fb=fb, fp=fp: (lambda e: e.activation(
                out=s_kact2[fp][:, fb * 512:fb * 512 + 512], in_=s_kr[:, :], func=AF.Square)))(),
                waits={"st": kv_used.get(fc - 2, 0)} if fb == 0 else None, inc="ss")
            kact_sv[(fc, fb)] = P.now("ss")
        ss_kact_f[fc] = P.now("ss")
        sv_kact_f[fc] = P.now("sv")
        k_used[fc] = P.now("st")

    def emit_rph(fc):
        fp = fc % 2
        for tl in range(4):
            for eh in range(2):
                bank = 2 * (tl % 2) + eh
                wt = merge_w({"sv": sv_mix_f[fc],
                              "ss": max(rs4_ss.get((fc, tl - 2), rs4_ss.get((fc - 1, tl + 2), 0)),
                                        ss_kact_f[fc] if tl % 2 == 1 else 0)},
                             ld_w["W_rT"])
                for cb in range(8):
                    P.emit("tensor", (lambda tl=tl, eh=eh, cb=cb, bank=bank, fp=fp: (lambda e: e.matmul(
                        PS[bank][:, :], s_xr2[fp][:, cb * 512 + tl * 128: cb * 512 + tl * 128 + 128],
                        s_wrT[:, cb * 1024 + eh * 512: cb * 1024 + eh * 512 + 512],
                        start=(cb == 0), stop=(cb == 7))))(),
                        waits=wt if cb == 0 else None, inc="st")
                P.emit("scalar", (lambda tl=tl, eh=eh, bank=bank: (lambda e: e.activation(
                    out=s_rtok4[:, tl * 1024 + eh * 512: tl * 1024 + eh * 512 + 512],
                    in_=PS[bank][:, :], func=AF.Sigmoid)))(),
                    waits={"st": P.now("st"), "sv": kv_v.get((fc - 1, tl), 0)}, inc="ss")
            rs4_ss[(fc, tl)] = P.now("ss")
        r_used[fc] = P.now("st")

    def emit_kvph(fc):
        fp = fc % 2
        for tl in range(4):
            bglob = 2 + 4 * fc + tl
            ring = tl % 2
            wvx = {"ds": spill_cnt[bglob - 1], "sv": P.now("sv"),
                   "do_": P.now("do_")}
            P.emit("sync", (lambda ring=ring, bglob=bglob: (lambda e: e.dma_start(
                out=s_x2r[ring][:, :], in_=d_x2.ap()[128 * (bglob - 1):128 * (bglob - 1) + 128, :])))(),
                waits=wvx, inc="dl", amt=16)
            dl_t = P.now("dl")
            for eh in range(2):
                bank = 4 + eh
                wt = {"sv": max(sv_kact_f[fc], kv_v.get((fc, tl - 1), kv_v.get((fc - 1, 3), 0))),
                      "ss": max(rs4_ss[(fc, tl)], ss_kact_f[fc]), "ld": ld_wvT}
                for fb in range(8):
                    P.emit("tensor", (lambda tl=tl, eh=eh, fb=fb, bank=bank, fp=fp: (lambda e: e.matmul(
                        PS[bank][:, :], s_kact2[fp][:, fb * 512 + tl * 128: fb * 512 + tl * 128 + 128],
                        s_wvT[:, fb * 1024 + eh * 512: fb * 1024 + eh * 512 + 512],
                        start=(fb == 0), stop=(fb == 7))))(),
                        waits=wt if fb == 0 else None, inc="st")
                P.emit("vector", (lambda tl=tl, eh=eh, bank=bank: (lambda e: e.tensor_mul(
                    s_ffn[:, 0:512], PS[bank][:, :],
                    s_rtok4[:, tl * 1024 + eh * 512: tl * 1024 + eh * 512 + 512])))(),
                    waits={"st": P.now("st"), "ss": rs4_ss[(fc, tl)]}, inc="sv")
                P.emit("vector", (lambda ring=ring, eh=eh: (lambda e: e.tensor_add(
                    s_x2r[ring][:, eh * 512:eh * 512 + 512], s_ffn[:, 0:512],
                    s_x2r[ring][:, eh * 512:eh * 512 + 512])))(),
                    waits={"dl": dl_t, "do_": P.now("do_")}, inc="sv")
            kv_v[(fc, tl)] = P.now("sv")
            P.emit("sync", (lambda ring=ring, bglob=bglob: (lambda e: e.dma_start(
                out=d_out[128 * (bglob - 2):128 * (bglob - 2) + 128, :], in_=s_x2r[ring][:, :])))(),
                waits={"sv": P.now("sv")}, inc="do_", amt=16)
        kv_used[fc] = P.now("st")

    emit_mix(0)
    emit_kph(0)
    emit_p4b_pair(10, 11)
    emit_rph(0)
    emit_mix(1)
    emit_p4b_pair(12, 13)
    emit_kvph(0)
    emit_kph(1)
    emit_p4b_pair(14, 15)
    emit_rph(1)
    emit_mix(2)
    emit_p4b_pair(16, 17)
    emit_kvph(1)
    emit_kph(2)
    emit_rph(2)
    emit_mix(3)
    emit_kvph(2)
    emit_kph(3)
    emit_rph(3)
    emit_kvph(3)
    do_all = P.now("do_")
    P.emit("sync", lambda e: e.wait_ge(sems["do_"], do_all))
    ctxD.close()
    ctxC2.close()

    OWN = {"vector": "sv", "scalar": "ss", "tensor": "st"}
    with nc.Block() as block:
        def make_runner(engname):
            def run(eng):
                lastw = {}
                own = OWN.get(engname)
                own_cnt = 0
                dma_cnt = {}
                for (e, waits, fn, incn, amt) in P.ops:
                    if e != engname:
                        continue
                    if own is not None and own_cnt > lastw.get(own, 0) and waits.get(own, 0) < own_cnt:
                        waits = dict(waits)
                        waits[own] = own_cnt
                    if own is None and incn is not None:
                        prev = dma_cnt.get(incn, 0)
                        if prev > waits.get(incn, 0):
                            waits = dict(waits)
                            waits[incn] = prev
                        dma_cnt[incn] = prev + amt
                    for s, v in waits.items():
                        if v > lastw.get(s, 0):
                            eng.wait_ge(sems[s], v)
                            lastw[s] = v
                    inst = fn(eng)
                    if incn is not None and inst is not None:
                        inst.then_inc(sems[incn], amt)
                        if incn == own:
                            own_cnt += amt
            return run
        block.vector(make_runner("vector"))
        block.scalar(make_runner("scalar"))
        block.tensor(make_runner("tensor"))
        block.sync(make_runner("sync"))
        block.gpsimd(make_runner("gpsimd"))

    ctx.close()
    return nc


def _host_consts(inp):
    c = {}
    bds = np.zeros((128, 128), np.float32)
    bdd = np.zeros((128, 128), np.float32)
    for h in range(8):
        bds[16 * h:16 * h + 16, 16 * h:16 * h + 16] = inp["L0_attn_W_strain"]
        bdd[16 * h:16 * h + 16, 16 * h:16 * h + 16] = inp["L0_attn_W_dmg"]
    c["BDs"], c["BDd"] = bds, bdd
    bd32 = np.zeros((128, 256), np.float32)
    for pi, W in enumerate((inp["L0_attn_W_bond"], inp["L0_attn_W_dmg_out"])):
        for v in range(4):
            for h in range(8):
                bd32[h * 16:h * 16 + 16, pi * 128 + v * 32 + v * 8 + h] = W[:, 0]
    c["BD32"] = bd32
    c["IDENT"] = np.eye(128, dtype=np.float32)
    c["relT"] = np.ascontiguousarray(inp["L0_attn_rel_pos_emb"][:64].T)
    c["W_pos"] = inp["L0_attn_W_pos"]
    c["b_dmgC"] = np.ascontiguousarray(np.tile(inp["L0_attn_b_dmg"].reshape(16, 1), (8, 1)))
    c["b_outC"] = np.ascontiguousarray(np.broadcast_to(inp["L0_attn_b_dmg_out"].reshape(1, 1), (128, 1)).copy())
    rep = np.zeros((16, 128), np.float32)
    for d in range(16):
        rep[d, d::16] = 1.0
    c["REP16"] = rep
    c["mu_r"] = np.ascontiguousarray(inp["L0_ffn_mu_r"].reshape(8, 128).T)
    c["mu_k"] = np.ascontiguousarray(inp["L0_ffn_mu_k"].reshape(8, 128).T)
    c["W_disp"] = inp["L0_attn_W_disp"]
    c["W_val"] = inp["L0_attn_W_val"]
    c["W_cproj"] = inp["L0_attn_W_cproj"]
    c["W_rT"] = np.ascontiguousarray(inp["L0_ffn_W_r"].T)
    c["W_kT"] = np.ascontiguousarray(inp["L0_ffn_W_k"].T)
    c["W_vT"] = np.ascontiguousarray(inp["L0_ffn_W_v"].T)
    return c


def kernel(**inputs):
    inputs = {k: np.ascontiguousarray(np.asarray(v), dtype=np.float32) for k, v in inputs.items()}
    x = inputs["L0_attn_x"]
    consts = _host_consts(inputs)
    mask_edge = np.zeros((128, 252), np.float32)
    for g in range(4):
        for js in range(16):
            for h in range(8):
                p = 8 * js + h
                j = 16 * g + js
                for trel in range(63):
                    mask_edge[p, g * 63 + trel] = 1.0 if j >= 63 - trel else 0.0
    mask_ones = np.ones((128, 252), np.float32)

    in_maps = []
    for core in range(NCORES):
        bb, s0 = core // 4, (core % 4) * 2048
        xs = np.zeros((2112, C), np.float32)
        xs[64:] = x[bb, s0:s0 + 2048]
        if s0 > 0:
            xs[:64] = x[bb, s0 - 64:s0]
        m = dict(consts)
        m["xs"] = xs
        m["EMASK"] = mask_edge if s0 == 0 else mask_ones
        in_maps.append(m)

    nc = build_nc()
    res = run_bass_kernel_spmd(nc, in_maps, core_ids=list(range(NCORES)))
    out = np.zeros((B, T, C), np.float32)
    for core in range(NCORES):
        bb, s0 = core // 4, (core % 4) * 2048
        out[bb, s0:s0 + 2048] = res.results[core]["out"]
    return out
